# version 09
# baseline (speedup 1.0000x reference)
import numpy as np
from contextlib import ExitStack

import concourse.bass as bass
import concourse.mybir as mybir
from concourse.bass_utils import run_bass_kernel_spmd

F32 = mybir.dt.float32
BF16 = mybir.dt.bfloat16
AF = mybir.ActivationFunctionType
ALU = mybir.AluOpType

B, S, D, HF = 4, 8192, 2048, 8192
T = 4096
C = 512
NCH = T // C
KT = D // 128
MT = D // 128
GT = HF // 128
EPS = 1e-5
SCALE = float(D) ** -0.5
FWS = float(HF) ** -0.5
NSLAB = 8
SMT = 8
NMG = 8

SB_BASE = 16512
OFF_W0 = SB_BASE
OFF_W1 = SB_BASE + 65536
OFF_POOL = SB_BASE + 131072
POOL_SZ = 73728
OFF_CONST = SB_BASE + 204800
CONST_SZ = 8032

UPTO = 99
TAPS = ()


class Sm:
    def __init__(self, h):
        self.h = h
        self.n = 0


class Alloc:
    def __init__(self, nc, off, size, tag):
        self.nc, self.off, self.end = nc, off, off + size
        self.tag = tag
        self.i = 0

    def t(self, name, shape, dtype):
        bpp = int(np.prod(shape[1:])) * (2 if dtype == BF16 else 4)
        bpp = (bpp + 63) // 64 * 64
        assert self.off + bpp <= self.end, (
            f"{self.tag}: {name} needs {bpp}B at {self.off}, end {self.end}")
        h = self.nc.alloc_sbuf_tensor_at(f"{self.tag}_{name}_{self.i}", list(shape),
                                         dtype, offset=self.off)
        self.i += 1
        self.off += bpp
        return h


def build_program(upto=99, taps=()):
    nc = bass.Bass("TRN2", target_bir_lowering=False)

    xT = nc.dram_tensor("xT", [D, T], F32, kind="ExternalInput").ap()
    wq = nc.dram_tensor("wq", [D, D], F32, kind="ExternalInput").ap()
    wqa = nc.dram_tensor("wqa", [D, D], F32, kind="ExternalInput").ap()
    wk = nc.dram_tensor("wk", [D, D], F32, kind="ExternalInput").ap()
    wka = nc.dram_tensor("wka", [D, D], F32, kind="ExternalInput").ap()
    wv = nc.dram_tensor("wv", [D, D], F32, kind="ExternalInput").ap()
    w1 = nc.dram_tensor("w1", [D, HF], F32, kind="ExternalInput").ap()
    w3 = nc.dram_tensor("w3", [D, HF], F32, kind="ExternalInput").ap()
    w2s = nc.dram_tensor("w2s", [HF, 256], F32, kind="ExternalInput").ap()
    consts = nc.dram_tensor("consts", [128, 114], F32, kind="ExternalInput").ap()
    h128 = nc.dram_tensor("h128", [128, 128], F32, kind="ExternalInput").ap()
    y = nc.dram_tensor("y", [D, T], F32, kind="ExternalOutput").ap()

    def split3(dram2d, kk):
        return dram2d.rearrange("(k p) c -> p k c", p=128)

    def sb3(sbt, kk, cc):
        return sbt[:, :].rearrange("p (k c) -> p k c", k=kk)

    def scratch(name, shape, dtype):
        return nc.dram_tensor(name, shape, dtype, kind="Internal").ap()

    q_d = scratch("q_d", [D, T], F32)
    arg_d = scratch("arg_d", [D, T], F32)
    q1l_d = scratch("q1l_d", [D, T], F32)
    k_d = scratch("k_d", [D, T], F32)
    p_d = scratch("p_d", [D, T], F32)
    mk_d = scratch("mk_d", [D, T], F32)
    barg_d = scratch("barg_d", [D, T], F32)
    k1l_d = scratch("k1l_d", [D, T], F32)
    v_d = scratch("v_d", [D, T], F32)
    pk_d = scratch("pk_d", [D, T], F32)
    x1_d = scratch("x1_d", [D, T], F32)
    h_d = scratch("h_d", [D, T], BF16)
    h2_d = scratch("h2_d", [D, T], BF16)
    hid_d = scratch("hid_d", [HF, T], BF16)
    w2pp_all = scratch("w2pp_all", [NMG * HF, 256], BF16)
    cc_in = [scratch(f"cc_in{i}", [128, 16], F32) for i in range(4)]
    cc_out = [scratch(f"cc_out{i}", [128, 16], F32) for i in range(4)]
    w2pp_mine = scratch("w2pp_mine", [HF, 256], BF16)
    scr_map = {"q_d": q_d, "arg_d": arg_d, "q1l_d": q1l_d, "k_d": k_d,
               "p_d": p_d, "mk_d": mk_d, "barg_d": barg_d, "k1l_d": k1l_d,
               "v_d": v_d, "pk_d": pk_d, "x1_d": x1_d, "h_d": h_d,
               "h2_d": h2_d, "hid_d": hid_d}
    tap_outs = {}
    for tname in taps:
        src = scr_map[tname]
        tap_outs[tname] = nc.dram_tensor(
            "tap_" + tname, list(src.tensor.shape), src.tensor.dtype,
            kind="ExternalOutput").ap()

    wbuf0 = nc.alloc_sbuf_tensor_at("wbuf0", [128, KT * D], BF16, offset=OFF_W0)
    wbuf1 = nc.alloc_sbuf_tensor_at("wbuf1", [128, KT * D], BF16, offset=OFF_W1)
    p0A = nc.alloc_sbuf_tensor_at("p0A", [128, 16384], F32, offset=OFF_W0)
    p0B = nc.alloc_sbuf_tensor_at("p0B", [128, 16384], F32, offset=OFF_W1)
    pgW = [(nc.alloc_sbuf_tensor_at(f"pgW1_{r}", [128, KT * 1024], BF16,
                                    offset=[OFF_W0, OFF_W1][r]),
            nc.alloc_sbuf_tensor_at(f"pgW3_{r}", [128, KT * 1024], BF16,
                                    offset=[OFF_W0, OFF_W1][r] + 32768))
           for r in range(2)]
    phW = [nc.alloc_sbuf_tensor_at(f"phW_{r}", [128, GT * 256], BF16,
                                   offset=[OFF_W0, OFF_W1][r]) for r in range(2)]

    ca = Alloc(nc, OFF_CONST, CONST_SZ, "cst")
    cst = ca.t("consts", [128, 114], F32)
    bqa_s = ca.t("bqa_s", [128, 16], F32)
    bka_s = ca.t("bka_s", [128, 16], F32)
    Q1c = ca.t("Q1c", [128, 16], F32)
    Pc = ca.t("Pc", [128, 16], F32)
    K1c = ca.t("K1c", [128, 16], F32)
    PKc = ca.t("PKc", [128, 16], F32)
    prevU = [ca.t(f"prevU{i}", [128, 16], F32) for i in range(4)]
    ccsnd = ca.t("ccsnd", [128, 16], F32)
    ccraw = ca.t("ccraw", [128, 16], F32)
    zeros = ca.t("zeros", [128, C], F32)
    ones_cb = ca.t("ones_cb", [128, 1], BF16)
    ones_cf = ca.t("ones_cf", [128, 1], F32)
    ones_r = ca.t("ones_r", [1, 128], F32)
    h128b = ca.t("h128b", [128, 128], BF16)

    def bias_col(idx, mi):
        return cst[:, idx * 16 + mi:idx * 16 + mi + 1]
    sendm = cst[:, 112:113]
    usem = cst[:, 113:114]

    ps = [nc.alloc_psum_tensor(f"ps{i}", [128, C], F32) for i in range(8)]

    streams = {k: [] for k in ("tensor", "vector", "scalar", "sync", "gpsimd")}

    def em(eng, f):
        streams[eng].append(f)

    def op(eng, f, sem=None, k=1):
        if sem is None:
            em(eng, f)
            return None
        sem.n += k
        tok = (sem, sem.n)
        em(eng, lambda e, f=f, h=sem.h, kk=k: f(e).then_inc(h, kk))
        return tok

    def w(eng, tok):
        if tok is None:
            return
        sem, v = tok
        if v <= 0:
            return
        em(eng, lambda e, h=sem.h, t=v: e.wait_ge(h, t))

    def last2(dct, key_new, key_old):
        return dct.get(key_new) or dct.get(key_old)

    with ExitStack() as st:
        def NS(name):
            return Sm(st.enter_context(nc.semaphore(name)))

        sPe, sSc, sVe, sCa, sCc = NS("sPe"), NS("sSc"), NS("sVe"), NS("sCa"), NS("sCc")
        sGp = NS("sGp")

        def tokPe():
            return (sPe, sPe.n)

        def tokVe():
            return (sVe, sVe.n)

        def tokSc():
            return (sSc, sSc.n)

        def tokCa():
            return (sCa, sCa.n)

        def tokGp():
            return (sGp, sGp.n)

        block = st.enter_context(nc.Block())

        def guard(eng):
            for tk in (tokPe(), tokVe(), tokSc(), tokCa(), tokGp()):
                w(eng, tk)

        sIni = NS("sIni")
        t_cst = op("sync", lambda e: e.dma_start(out=cst[:, :], in_=consts[:, :]), sIni, 16)
        op("gpsimd", lambda e: e.dma_start(out=h128b[:], in_=h128[:, :]), sIni, 16)
        for buf, val in ((zeros, 0.0), (Q1c, 0.0), (Pc, 0.0), (K1c, 0.0), (PKc, 0.0),
                         (ones_cb, 1.0), (ones_cf, 1.0), (ones_r, 1.0)):
            op("gpsimd", lambda e, b=buf, v=val: e.memset(b[:], v), sIni, 1)
        t_init = (sIni, sIni.n)
        w("vector", t_cst)
        op("vector", lambda e: e.tensor_scalar_mul(bqa_s[:], cst[:, 16:32], SCALE), sVe)
        op("vector", lambda e: e.tensor_scalar_mul(bka_s[:], cst[:, 48:64], SCALE), sVe)
        t_bias_ready = tokVe()

        w2mid_d = scratch("w2mid_d", [NMG, HF, 256], BF16)[0]
        sAL = [NS("sAL0")]
        sAM = [NS("sAM0"), NS("sAM1")]
        sBL = [NS("sBL0"), NS("sBL1")]
        sP0W = [NS("sP0W0"), NS("sP0W1")]
        sGat = NS("sGat")
        t_w2pp_done = []
        t_p0_pe = (sPe, 0)
        t_ag = None
        if upto >= 0:
            pa = Alloc(nc, OFF_POOL, POOL_SZ, "p0a")
            slabA = [pa.t(f"slA{r}", [128, GT * 256], BF16) for r in range(2)]
            obA = [pa.t(f"obA{r}", [128, 8 * 256], BF16) for r in range(2)]
            midb = [nc.alloc_sbuf_tensor_at(f"midb{r}", [64, 64 * 256], BF16,
                                            offset=OFF_W0 + r * 32768) for r in range(2)]
            obB = [nc.alloc_sbuf_tensor_at(f"obB{r}", [64, 8 * 512], BF16,
                                           offset=OFF_W1 + r * 8192) for r in range(2)]
            mmA, evA, stA = {}, {}, {}
            mmB, evB, stB = {}, {}, {}
            srcap = split3(w2s, GT)
            w("gpsimd", t_init)
            t_ld = op("gpsimd", lambda e, s_=srcap, d=sb3(slabA[0], GT, 256):
                      e.dma_start(out=d, in_=s_), sAL[0], 16)
            for gi in range(GT):
                grp = gi // 8
                bank = ps[gi % 2]
                w("tensor", t_ld)
                if gi == 0:
                    w("tensor", t_init)
                w("tensor", evA.get(gi - 2))
                t_mm = op("tensor", lambda e, b=bank, g=gi, sl=slabA[0]:
                          e.matmul(b[:, 0:256], h128b[:], sl[:, g * 256:(g + 1) * 256],
                                   start=True, stop=True), sPe)
                mmA[gi] = t_mm
                w("scalar", t_mm)
                if gi % 8 == 0:
                    w("scalar", stA.get(grp - 2))
                t_ev = op("scalar", lambda e, o=obA[(gi // 8) % 2], b=bank, g=gi:
                          e.activation(o[:, (g % 8) * 256:(g % 8 + 1) * 256], b[:, 0:256],
                                       AF.Identity, bias=0.0, scale=FWS), sSc)
                evA[gi] = t_ev
                if gi % 8 == 7:
                    w("sync", t_ev)
                    g0 = gi - 7
                    dstm = split3(w2mid_d[g0 * 128:(g0 + 8) * 128, :], 8)
                    stA[grp] = op("sync", lambda e, o=sb3(obA[(gi // 8) % 2], 8, 256),
                                  d=dstm: e.dma_start(out=d, in_=o), sAM[(gi // 8) % 2], 16)
            for hh in range(2):
                srcb = w2mid_d.rearrange("(g c) d -> g c d", c=128)[
                    :, hh * 64:(hh + 1) * 64, :]
                w("sync", stA.get(6))
                w("sync", stA.get(7))
                t_bl = op("sync", lambda e, s_=srcb,
                          d=midb[hh][0:64, :].rearrange("p (c d) -> p c d", c=64):
                          e.dma_start(out=d, in_=s_), sBL[hh], 16)
                for jj in range(32):
                    gj = hh * 32 + jj
                    grpb = gj // 8
                    bank = ps[2 + jj % 2]
                    w("tensor", t_bl)
                    w("tensor", evB.get(gj - 2))
                    t_mm = op("tensor", lambda e, b=bank, m=midb[hh], j2=jj:
                              e.matmul(b[0:64, :], h128b[0:64, 0:64],
                                       m[0:64, j2 * 512:(j2 + 1) * 512],
                                       start=True, stop=True), sPe)
                    mmB[(hh, jj)] = t_mm
                    w("vector", t_mm)
                    if gj % 8 == 0:
                        w("vector", stB.get(grpb - 2))
                    t_ev = op("vector", lambda e, o=obB[grpb % 2], b=bank, j2=jj:
                              e.tensor_copy(o[0:64, (j2 % 8) * 512:(j2 % 8 + 1) * 512],
                                            b[0:64, :]), sVe)
                    evB[gj] = t_ev
                    if gj % 8 == 7:
                        w("gpsimd", t_ev)
                        c0 = hh * 64 + (jj - 7) * 2
                        dstap = w2pp_mine.rearrange("(g c) d -> g c d", c=128)[
                            :, c0:c0 + 16, :]
                        stB[grpb] = op("gpsimd", lambda e, o=obB[grpb % 2][0:64, :]
                                       .rearrange("g (c d) -> g c d", c=16),
                                       d=dstap: e.dma_start(out=d, in_=o),
                                       sP0W[grpb % 2], 16)
            t_w2pp_done = [(sP0W[0], sP0W[0].n), (sP0W[1], sP0W[1].n)]
            t_p0_pe = tokPe()

        sWq, sWqa, sWk, sWka, sWv = NS("sWq"), NS("sWqa"), NS("sWk"), NS("sWka"), NS("sWv")

        def load_w2048(dst, srcap, sem, pre):
            for tk in pre:
                w("gpsimd", tk)
            src = split3(srcap, KT)
            return op("gpsimd", lambda e, d=sb3(dst, KT, D), s=src:
                      e.dma_start(out=d, in_=s), sem, 16)

        if upto >= 1:
            t_wq = load_w2048(wbuf0, wq, sWq, [t_p0_pe])
            t_wqa = load_w2048(wbuf1, wqa, sWqa,
                               list(t_w2pp_done))

        def ln_rows_and_bcast(rw, plain, t_stats_pe, prev_bcast_reads):
            mu, ms, var, rr, sr = rw["mu"], rw["ms"], rw["var"], rw["rr"], rw["sr"]
            w("vector", t_stats_pe)
            op("vector", lambda e, a=mu: e.tensor_scalar_mul(a[:], ps[2][0:1, 0:C], 1.0 / D), sVe)
            op("vector", lambda e, a=ms: e.tensor_scalar_mul(a[:], ps[3][0:1, 0:C], 1.0 / D), sVe)
            op("vector", lambda e, a=var, b=mu: e.tensor_mul(a[:], b[:], b[:]), sVe)
            op("vector", lambda e, a=var, b=ms: e.tensor_sub(a[:], b[:], a[:]), sVe)
            if plain:
                t_t = op("vector", lambda e, a=var: e.tensor_scalar_add(a[:], a[:], EPS), sVe)
            else:
                t_t = op("vector", lambda e, a=var: e.tensor_scalar(
                    a[:], a[:], 1.0 + EPS, EPS * EPS, ALU.mult, ALU.add), sVe)
            w("scalar", t_t)
            t_ln = op("scalar", lambda e, a=rr, b=var:
                      e.activation(a[:], b[:], AF.Ln, bias=0.0, scale=1.0), sSc)
            w("scalar", t_ln)
            t_rr = op("scalar", lambda e, a=rr:
                      e.activation(a[:], a[:], AF.Exp, bias=0.0, scale=-0.5), sSc)
            w("vector", t_rr)
            t_sr = op("vector", lambda e, a=sr, b=mu, c=rr: e.tensor_mul(a[:], b[:], c[:]), sVe)
            w("tensor", t_sr)
            w("tensor", prev_bcast_reads)
            op("tensor", lambda e, a=rr: e.matmul(ps[4][:, 0:C], ones_r[:, :], a[:, :],
                                                  start=True, stop=True), sPe)
            t_bc = op("tensor", lambda e, a=sr: e.matmul(ps[5][:, 0:C], ones_r[:, :], a[:, :],
                                                         start=True, stop=True), sPe)
            return t_bc

        pa = Alloc(nc, OFF_POOL, POOL_SZ, "pa")
        xbf = pa.t("xbf", [128, KT * C], BF16)
        hb = [pa.t(f"h{r}", [128, KT * C], BF16) for r in range(2)]
        qo = [pa.t(f"qo{r}", [128, C], F32) for r in range(2)]
        sqt = [pa.t(f"sq{r}", [128, C], BF16) for r in range(2)]
        t1b = [pa.t(f"t1{r}", [128, C], F32) for r in range(2)]
        t2b = [pa.t(f"t2{r}", [128, C], F32) for r in range(2)]
        rows = {n_: pa.t(n_, [1, C], F32) for n_ in ("mu", "ms", "var", "rr", "sr")}
        sXL = NS("sXL")
        sHSP = [NS("sHSP0"), NS("sHSP1")]
        sQSP = [NS("sQSP0"), NS("sQSP1")]
        if upto >= 1:
            h_done = {}
            t_pa_mm = {}
            h_sp = {}
            q_sp = {}
            ev_q = {}
            sq_rd = {}
            prev_xbf_users = None
            prev_stats_read = None
            prev_ln_reads = None
            for n in range(NCH):
                src = split3(xT[:, n * C:(n + 1) * C], KT)
                if n == 0:
                    w("gpsimd", t_init)
                    w("gpsimd", t_p0_pe)
                    w("vector", t_w2pp_done[0] if t_w2pp_done else None)
                    w("vector", t_w2pp_done[1] if t_w2pp_done else None)
                else:
                    w("gpsimd", prev_xbf_users)
                    w("gpsimd", prev_stats_read)
                t_x = op("gpsimd", lambda e, s=src, d=sb3(xbf, KT, C):
                         e.dma_start(out=d, in_=s), sXL, 16)
                if n == 0:
                    w("scalar", t_w2pp_done[0])
                    w("scalar", t_w2pp_done[1])
                w("scalar", t_x)
                w("tensor", t_x)
                if n > 0:
                    w("tensor", prev_ln_reads)
                for mi in range(MT):
                    w("scalar", last2(sq_rd, (n, mi - 2), (n - 1, mi + MT - 2)))
                    t_sq = op("scalar", lambda e, m=mi, r=sqt[mi % 2]:
                              e.activation(r[:], xbf[:, m * C:(m + 1) * C], AF.Square,
                                           bias=0.0, scale=1.0), sSc)
                    op("tensor", lambda e, m=mi: e.matmul(
                        ps[2][0:1, 0:C], ones_cb[:, :], xbf[:, m * C:(m + 1) * C],
                        start=(m == 0), stop=(m == MT - 1)), sPe)
                    w("tensor", t_sq)
                    t_stats = op("tensor", lambda e, m=mi, r=sqt[mi % 2]: e.matmul(
                        ps[3][0:1, 0:C], ones_cb[:, :], r[:],
                        start=(m == 0), stop=(m == MT - 1)), sPe)
                    sq_rd[(n, mi)] = t_stats
                prev_stats_read = t_stats
                t_bc = ln_rows_and_bcast(rows, False, t_stats, prev_ln_reads)
                hcur = hb[n % 2]
                for mi in range(MT):
                    w("vector", t_bc)
                    if n >= 2:
                        w("vector", t_pa_mm.get((n - 2, MT - 1)))
                        w("vector", h_sp.get(n - 2))
                    op("vector", lambda e, m=mi, r=t1b[mi % 2]:
                       e.tensor_mul(r[:], xbf[:, m * C:(m + 1) * C], ps[4][:, 0:C]), sVe)
                    t_t2 = op("vector", lambda e, a=t1b[mi % 2], b=t2b[mi % 2]:
                              e.tensor_sub(b[:], a[:], ps[5][:, 0:C]), sVe)
                    w("scalar", t_t2)
                    t_h = op("scalar", lambda e, m=mi, b=t2b[mi % 2], hc=hcur:
                             e.activation(hc[:, m * C:(m + 1) * C], b[:], AF.Identity,
                                          bias=bias_col(6, m), scale=bias_col(5, m)), sSc)
                    h_done[(n, mi)] = t_h
                prev_xbf_users = tokVe()
                prev_ln_reads = tokVe()
                w("sync", h_done[(n, MT - 1)])
                if n >= 2:
                    w("sync", t_pa_mm.get((n - 2, MT - 1)))
                dsth = split3(h_d[:, n * C:(n + 1) * C], KT)
                h_sp[n] = op("sync", lambda e, d=dsth, hc=sb3(hcur, KT, C):
                             e.dma_start(out=d, in_=hc), sHSP[n % 2], 16)
                if n == 0:
                    w("tensor", t_wq)
                for mi in range(MT):
                    idx = n * MT + mi
                    bank = [ps[0], ps[1], ps[6], ps[7]][mi % 4]
                    w("tensor", h_done[(n, MT - 1)])
                    w("tensor", ev_q.get(idx - 4))
                    for ki in range(KT):
                        tok = op("tensor", lambda e, k=ki, m=mi, b=bank, hc=hcur:
                                 e.matmul(b[:], wbuf0[:, k * D + m * 128:k * D + (m + 1) * 128],
                                          hc[:, k * C:(k + 1) * C],
                                          start=(k == 0), stop=(k == KT - 1)),
                                 sPe if ki == KT - 1 else None)
                    t_pa_mm[(n, mi)] = tok
                    w("scalar", tok)
                    w("scalar", q_sp.get(idx - 2))
                    t_ev = op("scalar", lambda e, b=bank, m=mi, r=qo[mi % 2]:
                              e.activation(r[:], b[:], AF.Identity, bias=bias_col(0, m),
                                           scale=1.0), sSc)
                    ev_q[idx] = t_ev
                    w("sync", t_ev)
                    q_sp[idx] = op("sync", lambda e, m=mi, r=qo[mi % 2], nn=n:
                                   e.dma_start(out=q_d[m * 128:(m + 1) * 128,
                                                       nn * C:(nn + 1) * C],
                                               in_=r[:]), sQSP[mi % 2], 16)
            t_pa_pe = tokPe()
            pa_h_sp = dict(h_sp)
            pa_q_sp = dict(q_sp)
            w("gpsimd", t_w2pp_done[0])
            w("gpsimd", t_w2pp_done[1])
            t_ag = op("gpsimd", lambda e: e.collective_compute(
                "AllGather", ALU.bypass,
                replica_groups=[[0, 1, 2, 3, 4, 5, 6, 7]],
                ins=[w2pp_mine.tensor.ap().opt()],
                outs=[w2pp_all.tensor.ap().opt()]), sGat, 1)

        def scan_stage(sid, n, mi, data_sb, carry, out_sb, ct):
            w("vector", ct.get((sid, n - 1, mi)))
            t_s = op("vector", lambda e, d_=data_sb, o=out_sb, cc=carry, m=mi:
                     e.tensor_tensor_scan(o[:], d_[:], zeros[:], cc[:, m:m + 1],
                                          ALU.add, ALU.add), sCa)
            w("vector", t_s)
            t_c = op("vector", lambda e, o=out_sb, cc=carry, m=mi:
                     e.tensor_copy(cc[:, m:m + 1], o[:, C - 1:C]), sCa)
            ct[(sid, n, mi)] = t_c
            return t_s, t_c

        sQBF = [NS("sQBF0"), NS("sQBF1")]
        sASP = [NS("sASP0"), NS("sASP1")]
        sQ1SP = [NS("sQ1SP0"), NS("sQ1SP1")]
        if upto >= 2:
            pa = Alloc(nc, OFF_POOL, POOL_SZ, "pa2")
            qbf = [pa.t(f"qbf{r}", [128, KT * C], BF16) for r in range(2)]
            qwt = [pa.t(f"qwt{r}", [128, C], F32) for r in range(2)]
            q1o = [pa.t(f"q1o{r}", [128, C], F32) for r in range(2)]
            ev_a, a_sp, q1_sp, ct, mm2, scand = {}, {}, {}, {}, {}, {}
            for n in range(NCH):
                src = split3(q_d[:, n * C:(n + 1) * C], KT)
                if n == 0:
                    guard("gpsimd")
                w("gpsimd", pa_q_sp.get(n * MT + MT - 2))
                w("gpsimd", pa_q_sp.get(n * MT + MT - 1))
                w("gpsimd", mm2.get((n - 2, MT - 1)))
                t_q = op("gpsimd", lambda e, s=src, b=sb3(qbf[n % 2], KT, C):
                         e.dma_start(out=b, in_=s), sQBF[n % 2], 16)
                w("tensor", t_q)
                if n == 0:
                    w("tensor", t_wqa)
                for mi in range(MT):
                    idx = n * MT + mi
                    bank = ps[mi % 4]
                    w("tensor", ev_a.get(idx - 4))
                    for ki in range(KT):
                        tokm = op("tensor", lambda e, k=ki, m=mi, b=bank, r=qbf[n % 2]:
                                  e.matmul(b[:], wbuf1[:, k * D + m * 128:k * D + (m + 1) * 128],
                                           r[:, k * C:(k + 1) * C],
                                           start=(k == 0), stop=(k == KT - 1)),
                                  sPe if ki == KT - 1 else None)
                    mm2[(n, mi)] = tokm
                    w("scalar", tokm)
                    if idx == 0:
                        w("scalar", t_bias_ready)
                    w("scalar", a_sp.get(idx - 2))
                    w("scalar", scand.get(idx - 2))
                    t_qw = op("scalar", lambda e, b=bank, m=mi, o=qwt[mi % 2]:
                              e.activation(o[:], b[:], AF.Exp,
                                           bias=bqa_s[:, m:m + 1], scale=SCALE), sSc)
                    ev_a[idx] = t_qw
                    w("sync", t_qw)
                    a_sp[idx] = op("sync", lambda e, o=qwt[mi % 2], m=mi, nn=n:
                                   e.dma_start(out=arg_d[m * 128:(m + 1) * 128,
                                                         nn * C:(nn + 1) * C],
                                               in_=o[:]), sASP[mi % 2], 16)
                    w("vector", t_qw)
                    if idx == 0:
                        w("vector", t_init)
                    w("vector", q1_sp.get(idx - 2))
                    t_s, t_c = scan_stage(0, n, mi, qwt[mi % 2], Q1c, q1o[mi % 2], ct)
                    scand[idx] = t_s
                    w("sync", t_s)
                    q1_sp[idx] = op("sync", lambda e, o=q1o[mi % 2], m=mi, nn=n:
                                    e.dma_start(out=q1l_d[m * 128:(m + 1) * 128,
                                                          nn * C:(nn + 1) * C],
                                                in_=o[:]), sQ1SP[mi % 2], 16)
            t_pa2_carry = ct[(0, NCH - 1, MT - 1)]
            t_pa2_pe = tokPe()
            pa2_a_sp = dict(a_sp)
            pa2_q1_sp = dict(q1_sp)

        sCcD = NS("sCcD")
        cc_prev = [None]

        def pair_exchange(cid, carry_sb, prev_sb, t_carry_final):
            w("vector", t_carry_final)
            w("vector", cc_prev[0])
            t_snd = op("vector", lambda e, c_=carry_sb:
                       e.tensor_scalar_mul(ccsnd[:], c_[:], sendm), sVe)
            w("gpsimd", t_snd)
            t_d1 = op("gpsimd", lambda e, i=cid: e.dma_start(out=cc_in[i][:, :], in_=ccsnd[:]),
                      sCcD, 16)
            cc_prev[0] = t_d1
            w("gpsimd", t_d1)
            t_cc = op("gpsimd", lambda e, i=cid: e.collective_compute(
                "AllReduce", ALU.add,
                replica_groups=[[0, 1], [2, 3], [4, 5], [6, 7]],
                ins=[cc_in[i].tensor.ap().opt()], outs=[cc_out[i].tensor.ap().opt()]), sCc, 1)
            w("gpsimd", t_cc)
            t_d2 = op("gpsimd", lambda e, i=cid: e.dma_start(out=ccraw[:], in_=cc_out[i][:, :]),
                      sCcD, 16)
            w("vector", t_d2)
            return op("vector", lambda e, p=prev_sb:
                      e.tensor_scalar_mul(p[:], ccraw[:], usem), sVe)

        if upto >= 3:
            t_q1prev = pair_exchange(0, Q1c, prevU[0], t_pa2_carry)
            t_wk = load_w2048(wbuf0, wk, sWk, [t_pa_pe])

        sLDa = [NS(f"sLDa{i}") for i in range(4)]
        sLDb = [NS(f"sLDb{i}") for i in range(4)]
        sLDc = [NS(f"sLDc{i}") for i in range(4)]
        sLDd = [NS(f"sLDd{i}") for i in range(4)]
        sLDe = [NS(f"sLDe{i}") for i in range(4)]

        sHL = [NS("sHL0"), NS("sHL1")]
        sLQ, sLA, sLQ1 = sLDa, sLDb, sLDc
        sKSP = [NS("sKSP0"), NS("sKSP1")]
        sPSP = [NS("sPSP0"), NS("sPSP1")]
        if upto >= 3:
            pa = Alloc(nc, OFF_POOL, POOL_SZ, "pb")
            hbuf = [pa.t(f"h{r}", [128, KT * C], BF16) for r in range(2)]
            R4 = 4
            qin = [pa.t(f"q{r}", [128, C], F32) for r in range(R4)]
            ain = [pa.t(f"a{r}", [128, C], F32) for r in range(R4)]
            q1in = [pa.t(f"q1{r}", [128, C], F32) for r in range(R4)]
            ko = [pa.t(f"ko{r}", [128, C], F32) for r in range(2)]
            ev_k, k_sp, p_sp, ct = {}, {}, {}, {}
            mm3, d_t, n1_t, div_t, scand = {}, {}, {}, {}, {}
            ldq, lda, ldq1 = {}, {}, {}

            def spill_p(j):
                jm, jn = j % MT, j // MT
                w("gpsimd", scand[j])
                p_sp[j] = op("gpsimd", lambda e, o=qin[jm % R4], m=jm, nn=jn:
                             e.dma_start(out=p_d[m * 128:(m + 1) * 128,
                                                 nn * C:(nn + 1) * C], in_=o[:]),
                             sPSP[j % 2], 16)

            for n in range(NCH):
                srch = split3(h_d[:, n * C:(n + 1) * C], KT)
                if n == 0:
                    guard("sync")
                    w("sync", pa_h_sp.get(NCH - 2))
                    w("sync", pa_h_sp.get(NCH - 1))
                w("sync", mm3.get((n - 2, MT - 1)))
                t_h = op("sync", lambda e, s=srch, b=sb3(hbuf[n % 2], KT, C):
                         e.dma_start(out=b, in_=s), sHL[n % 2], 16)
                for mi in range(MT):
                    idx = n * MT + mi
                    if n == 0:
                        w("sync", pa_q_sp.get(NCH * MT - 2))
                        w("sync", pa_q_sp.get(NCH * MT - 1))
                        w("sync", pa2_a_sp.get(NCH * MT - 2))
                        w("sync", pa2_a_sp.get(NCH * MT - 1))
                        w("sync", pa2_q1_sp.get(NCH * MT - 2))
                        w("sync", pa2_q1_sp.get(NCH * MT - 1))
                    w("sync", p_sp.get(idx - R4))
                    jj4 = idx - R4
                    if jj4 >= 0:
                        w("sync", ct.get((1, jj4 // MT, jj4 % MT)))
                    w("sync", n1_t.get(idx - R4))
                    w("sync", div_t.get(idx - R4))
                    ldq[idx] = op("sync", lambda e, b=qin[mi % R4], m=mi, nn=n:
                                  e.dma_start(out=b[:], in_=q_d[m * 128:(m + 1) * 128,
                                                              nn * C:(nn + 1) * C]),
                                  sLQ[mi % R4], 16)
                    lda[idx] = op("sync", lambda e, b=ain[mi % R4], m=mi, nn=n:
                                  e.dma_start(out=b[:], in_=arg_d[m * 128:(m + 1) * 128,
                                                                 nn * C:(nn + 1) * C]),
                                  sLA[mi % R4], 16)
                    ldq1[idx] = op("sync", lambda e, b=q1in[mi % R4], m=mi, nn=n:
                                   e.dma_start(out=b[:], in_=q1l_d[m * 128:(m + 1) * 128,
                                                                   nn * C:(nn + 1) * C]),
                                   sLQ1[mi % R4], 16)
                    bank = ps[mi % 4]
                    w("tensor", t_h)
                    if n == 0 and mi == 0:
                        w("tensor", t_wk)
                    w("tensor", ev_k.get(idx - 4))
                    for ki in range(KT):
                        tokm = op("tensor", lambda e, k=ki, m=mi, b=bank, r=hbuf[n % 2]:
                                  e.matmul(b[:], wbuf0[:, k * D + m * 128:k * D + (m + 1) * 128],
                                           r[:, k * C:(k + 1) * C],
                                           start=(k == 0), stop=(k == KT - 1)),
                                  sPe if ki == KT - 1 else None)
                    mm3[(n, mi)] = tokm
                    w("scalar", tokm)
                    w("scalar", k_sp.get(idx - 2))
                    t_evk = op("scalar", lambda e, b=bank, m=mi, r=ko[mi % 2]:
                               e.activation(r[:], b[:], AF.Identity, bias=bias_col(2, m),
                                            scale=1.0), sSc)
                    ev_k[idx] = t_evk
                    w("gpsimd", t_evk)
                    k_sp[idx] = op("gpsimd", lambda e, r=ko[mi % 2], m=mi, nn=n:
                                   e.dma_start(out=k_d[m * 128:(m + 1) * 128,
                                                       nn * C:(nn + 1) * C], in_=r[:]),
                                   sKSP[mi % 2], 16)
                    w("scalar", ldq1[idx])
                    if idx == 0:
                        w("scalar", t_q1prev)
                    op("scalar", lambda e, b=q1in[mi % R4], m=mi:
                       e.activation(b[:], b[:], AF.Ln, bias=prevU[0][:, m:m + 1],
                                    scale=1.0), sSc)
                    t_d = op("scalar", lambda e, b=q1in[mi % R4]:
                             e.activation(b[:], b[:], AF.Exp, bias=0.0, scale=-1.0), sSc)
                    d_t[idx] = t_d
                    w("vector", lda[idx])
                    w("vector", ldq[idx])
                    t_n1 = op("vector", lambda e, a=ain[mi % R4], q_=qin[mi % R4]:
                              e.tensor_mul(q_[:], a[:], q_[:]), sVe)
                    n1_t[idx] = t_n1
                    w("vector", t_d)
                    t_dv = op("vector", lambda e, q_=qin[mi % R4], l=q1in[mi % R4]:
                              e.tensor_mul(q_[:], q_[:], l[:]), sVe)
                    div_t[idx] = t_dv
                    t_s, t_c = scan_stage(1, n, mi, qin[mi % R4], Pc, qin[mi % R4], ct)
                    scand[idx] = t_s
                    if idx > 0:
                        spill_p(idx - 1)
            spill_p(NCH * MT - 1)
            t_pb_carry = ct[(1, NCH - 1, MT - 1)]
            t_pb_pe = tokPe()
            pb_k_sp = dict(k_sp)
            pb_p_sp = dict(p_sp)
            pb_last_lds = [dd[NCH * MT - 4 + i] for dd in (ldq, lda, ldq1)
                           for i in range(4)]

        if upto >= 4:
            t_pprev = pair_exchange(1, Pc, prevU[1], t_pb_carry)
            t_wka = load_w2048(wbuf1, wka, sWka, [t_pa2_pe])

        sLP, sLK = sLDd, sLDe
        sMKSP = [NS("sMKSP0"), NS("sMKSP1")]
        sBSP = [NS("sBSP0"), NS("sBSP1")]
        sK1SP = [NS("sK1SP0"), NS("sK1SP1")]
        if upto >= 4:
            pa = Alloc(nc, OFF_POOL, POOL_SZ, "pc")
            mkb = [pa.t(f"mkb{r}", [128, KT * C], BF16) for r in range(2)]
            R4 = 4
            pin = [pa.t(f"p{r}", [128, C], F32) for r in range(R4)]
            kin = [pa.t(f"k{r}", [128, C], F32) for r in range(R4)]
            kwt = [pa.t(f"kw{r}", [128, C], F32) for r in range(2)]
            k1o = [pa.t(f"k1o{r}", [128, C], F32) for r in range(2)]
            ev_b, b_sp, mk_sp, k1_sp, ct = {}, {}, {}, {}, {}
            mm4, mkv, mkc, kw_t, scand = {}, {}, {}, {}, {}
            ldp, ldk = {}, {}

            def pc_mk_build(n):
                for mi in range(MT):
                    idx = n * MT + mi
                    if idx == 0:
                        guard("sync")
                        w("sync", pb_p_sp.get(NCH * MT - 2))
                        w("sync", pb_p_sp.get(NCH * MT - 1))
                        w("sync", pb_k_sp.get(NCH * MT - 2))
                        w("sync", pb_k_sp.get(NCH * MT - 1))
                        for tk in pb_last_lds:
                            w("sync", tk)
                    w("sync", mkv.get(idx - R4))
                    w("sync", mk_sp.get(idx - R4))
                    w("sync", mkc.get(idx - R4))
                    ldp[idx] = op("sync", lambda e, b=pin[mi % R4], m=mi, nn=n:
                                  e.dma_start(out=b[:], in_=p_d[m * 128:(m + 1) * 128,
                                                              nn * C:(nn + 1) * C]),
                                  sLP[mi % R4], 16)
                    ldk[idx] = op("sync", lambda e, b=kin[mi % R4], m=mi, nn=n:
                                  e.dma_start(out=b[:], in_=k_d[m * 128:(m + 1) * 128,
                                                              nn * C:(nn + 1) * C]),
                                  sLK[mi % R4], 16)
                    w("vector", ldp[idx])
                    w("vector", ldk[idx])
                    if idx == 0:
                        w("vector", t_pprev)
                    t_mk = op("vector", lambda e, p=pin[mi % R4], k_=kin[mi % R4], m=mi:
                              e.scalar_tensor_tensor(k_[:], p[:], prevU[1][:, m:m + 1], k_[:],
                                                     ALU.add, ALU.mult), sVe)
                    mkv[idx] = t_mk
                    w("gpsimd", t_mk)
                    mk_sp[idx] = op("gpsimd", lambda e, o=kin[mi % R4], m=mi, nn=n:
                                    e.dma_start(out=mk_d[m * 128:(m + 1) * 128,
                                                         nn * C:(nn + 1) * C], in_=o[:]),
                                    sMKSP[mi % 2], 16)
                    w("scalar", t_mk)
                    if mi == 0:
                        w("scalar", mm4.get((n - 2, MT - 1)))
                    mkc[idx] = op("scalar", lambda e, s=kin[mi % R4], b=mkb[n % 2], m=mi:
                                  e.activation(b[:, m * C:(m + 1) * C], s[:], AF.Identity,
                                               bias=0.0, scale=1.0), sSc)

            pc_mk_build(0)
            for n in range(NCH):
                if n + 1 < NCH:
                    pc_mk_build(n + 1)
                for mi in range(MT):
                    idx = n * MT + mi
                    bank = ps[mi % 4]
                    w("tensor", mkc[n * MT + MT - 1])
                    if n == 0 and mi == 0:
                        w("tensor", t_wka)
                    w("tensor", ev_b.get(idx - 4))
                    for ki in range(KT):
                        tokm = op("tensor", lambda e, k=ki, m=mi, b=bank, r=mkb[n % 2]:
                                  e.matmul(b[:], wbuf1[:, k * D + m * 128:k * D + (m + 1) * 128],
                                           r[:, k * C:(k + 1) * C],
                                           start=(k == 0), stop=(k == KT - 1)),
                                  sPe if ki == KT - 1 else None)
                    mm4[(n, mi)] = tokm
                    w("scalar", tokm)
                    w("scalar", b_sp.get(idx - 2))
                    w("scalar", scand.get(idx - 2))
                    t_kw = op("scalar", lambda e, b=bank, m=mi, o=kwt[mi % 2]:
                              e.activation(o[:], b[:], AF.Exp,
                                           bias=bka_s[:, m:m + 1], scale=SCALE), sSc)
                    ev_b[idx] = t_kw
                    kw_t[idx] = t_kw
                    w("sync", t_kw)
                    b_sp[idx] = op("sync", lambda e, r=kwt[mi % 2], m=mi, nn=n:
                                   e.dma_start(out=barg_d[m * 128:(m + 1) * 128,
                                                          nn * C:(nn + 1) * C], in_=r[:]),
                                   sBSP[mi % 2], 16)
                    w("vector", t_kw)
                    w("vector", k1_sp.get(idx - 2))
                    t_s, t_c = scan_stage(2, n, mi, kwt[mi % 2], K1c, k1o[mi % 2], ct)
                    scand[idx] = t_s
                    w("sync", t_s)
                    k1_sp[idx] = op("sync", lambda e, o=k1o[mi % 2], m=mi, nn=n:
                                    e.dma_start(out=k1l_d[m * 128:(m + 1) * 128,
                                                          nn * C:(nn + 1) * C], in_=o[:]),
                                    sK1SP[mi % 2], 16)
            t_pc_carry = ct[(2, NCH - 1, MT - 1)]
            t_pc_pe = tokPe()
            pc_mk_sp = dict(mk_sp)
            pc_b_sp = dict(b_sp)
            pc_k1_sp = dict(k1_sp)
            pc_last_lds = [dd[NCH * MT - 4 + i] for dd in (ldp, ldk)
                           for i in range(4)]

        if upto >= 5:
            t_k1prev = pair_exchange(2, K1c, prevU[2], t_pc_carry)
            t_wv = load_w2048(wbuf0, wv, sWv, [t_pb_pe])

        sHL2 = [NS("sHL20"), NS("sHL21")]
        sLMK, sLB, sLK1 = sLDa, sLDb, sLDc
        sVSP = [NS("sVSP0"), NS("sVSP1")]
        sPKSP = [NS("sPKSP0"), NS("sPKSP1")]
        if upto >= 5:
            pa = Alloc(nc, OFF_POOL, POOL_SZ, "pd")
            hbuf2 = [pa.t(f"h{r}", [128, KT * C], BF16) for r in range(2)]
            R4 = 4
            mkin = [pa.t(f"mk{r}", [128, C], F32) for r in range(R4)]
            bain = [pa.t(f"ba{r}", [128, C], F32) for r in range(R4)]
            k1in = [pa.t(f"k1{r}", [128, C], F32) for r in range(R4)]
            vo = [pa.t(f"vo{r}", [128, C], F32) for r in range(2)]
            ev_v, v_sp, pk_sp, ct = {}, {}, {}, {}
            mm5, d2_t, n2_t, div2_t, scand = {}, {}, {}, {}, {}
            ldmk, ldba, ldk1 = {}, {}, {}

            def spill_pk(j):
                jm, jn = j % MT, j // MT
                w("gpsimd", scand[j])
                pk_sp[j] = op("gpsimd", lambda e, o=mkin[jm % R4], m=jm, nn=jn:
                              e.dma_start(out=pk_d[m * 128:(m + 1) * 128,
                                                   nn * C:(nn + 1) * C], in_=o[:]),
                              sPKSP[j % 2], 16)

            for n in range(NCH):
                srch = split3(h_d[:, n * C:(n + 1) * C], KT)
                if n == 0:
                    guard("sync")
                w("sync", mm5.get((n - 2, MT - 1)))
                t_h = op("sync", lambda e, s=srch, b=sb3(hbuf2[n % 2], KT, C):
                         e.dma_start(out=b, in_=s), sHL2[n % 2], 16)
                for mi in range(MT):
                    idx = n * MT + mi
                    if idx == 0:
                        w("sync", pc_mk_sp.get(NCH * MT - 2))
                        w("sync", pc_mk_sp.get(NCH * MT - 1))
                        w("sync", pc_b_sp.get(NCH * MT - 2))
                        w("sync", pc_b_sp.get(NCH * MT - 1))
                        w("sync", pc_k1_sp.get(NCH * MT - 2))
                        w("sync", pc_k1_sp.get(NCH * MT - 1))
                        for tk in pc_last_lds:
                            w("sync", tk)
                    w("sync", pk_sp.get(idx - R4))
                    jj4 = idx - R4
                    if jj4 >= 0:
                        w("sync", ct.get((3, jj4 // MT, jj4 % MT)))
                    w("sync", n2_t.get(idx - R4))
                    w("sync", div2_t.get(idx - R4))
                    ldmk[idx] = op("sync", lambda e, b=mkin[mi % R4], m=mi, nn=n:
                                   e.dma_start(out=b[:], in_=mk_d[m * 128:(m + 1) * 128,
                                                                  nn * C:(nn + 1) * C]),
                                   sLMK[mi % R4], 16)
                    ldba[idx] = op("sync", lambda e, b=bain[mi % R4], m=mi, nn=n:
                                   e.dma_start(out=b[:], in_=barg_d[m * 128:(m + 1) * 128,
                                                                    nn * C:(nn + 1) * C]),
                                   sLB[mi % R4], 16)
                    ldk1[idx] = op("sync", lambda e, b=k1in[mi % R4], m=mi, nn=n:
                                   e.dma_start(out=b[:], in_=k1l_d[m * 128:(m + 1) * 128,
                                                                   nn * C:(nn + 1) * C]),
                                   sLK1[mi % R4], 16)
                    bank = ps[mi % 4]
                    w("tensor", t_h)
                    if n == 0 and mi == 0:
                        w("tensor", t_wv)
                    w("tensor", ev_v.get(idx - 4))
                    for ki in range(KT):
                        tokm = op("tensor", lambda e, k=ki, m=mi, b=bank, r=hbuf2[n % 2]:
                                  e.matmul(b[:], wbuf0[:, k * D + m * 128:k * D + (m + 1) * 128],
                                           r[:, k * C:(k + 1) * C],
                                           start=(k == 0), stop=(k == KT - 1)),
                                  sPe if ki == KT - 1 else None)
                    mm5[(n, mi)] = tokm
                    w("scalar", tokm)
                    w("scalar", v_sp.get(idx - 2))
                    t_evv = op("scalar", lambda e, b=bank, m=mi, r=vo[mi % 2]:
                               e.activation(r[:], b[:], AF.Identity, bias=bias_col(4, m),
                                            scale=1.0), sSc)
                    ev_v[idx] = t_evv
                    w("gpsimd", t_evv)
                    v_sp[idx] = op("gpsimd", lambda e, r=vo[mi % 2], m=mi, nn=n:
                                   e.dma_start(out=v_d[m * 128:(m + 1) * 128,
                                                       nn * C:(nn + 1) * C], in_=r[:]),
                                   sVSP[mi % 2], 16)
                    w("scalar", ldk1[idx])
                    if idx == 0:
                        w("scalar", t_k1prev)
                    op("scalar", lambda e, b=k1in[mi % R4], m=mi:
                       e.activation(b[:], b[:], AF.Ln, bias=prevU[2][:, m:m + 1],
                                    scale=1.0), sSc)
                    t_d2 = op("scalar", lambda e, b=k1in[mi % R4]:
                              e.activation(b[:], b[:], AF.Exp, bias=0.0, scale=-1.0), sSc)
                    d2_t[idx] = t_d2
                    w("vector", ldba[idx])
                    w("vector", ldmk[idx])
                    t_n2 = op("vector", lambda e, a=bain[mi % R4], mkk=mkin[mi % R4]:
                              e.tensor_mul(mkk[:], a[:], mkk[:]), sVe)
                    n2_t[idx] = t_n2
                    w("vector", t_d2)
                    t_dv = op("vector", lambda e, mkk=mkin[mi % R4], l=k1in[mi % R4]:
                              e.tensor_mul(mkk[:], mkk[:], l[:]), sVe)
                    div2_t[idx] = t_dv
                    t_s, t_c = scan_stage(3, n, mi, mkin[mi % R4], PKc, mkin[mi % R4], ct)
                    scand[idx] = t_s
                    if idx > 0:
                        spill_pk(idx - 1)
            spill_pk(NCH * MT - 1)
            t_pd_carry = ct[(3, NCH - 1, MT - 1)]
            t_pd_pe = tokPe()
            pd_v_sp = dict(v_sp)
            pd_pk_sp = dict(pk_sp)

        if upto >= 6:
            t_pkprev = pair_exchange(3, PKc, prevU[3], t_pd_carry)

        sSLAB = [NS("sSLAB0"), NS("sSLAB1")]

        def load_slab(s_, reg, pre):
            for tk in pre:
                w("gpsimd", tk)
            w1s = split3(w1[:, s_ * 1024:(s_ + 1) * 1024], KT)
            w3s = split3(w3[:, s_ * 1024:(s_ + 1) * 1024], KT)
            op("gpsimd", lambda e, ss=w1s, d=sb3(pgW[reg][0], KT, 1024):
               e.dma_start(out=d, in_=ss), sSLAB[reg], 16)
            return op("gpsimd", lambda e, ss=w3s, d=sb3(pgW[reg][1], KT, 1024):
                      e.dma_start(out=d, in_=ss), sSLAB[reg], 16)

        sLX = [NS("sLX0"), NS("sLX1")]
        sLV = [NS("sLV0"), NS("sLV1")]
        sLPK = [NS("sLPK0"), NS("sLPK1")]
        sX1SP = NS("sX1SP")
        sH2SP = [NS("sH2SP0"), NS("sH2SP1")]
        if upto >= 6:
            t_slab = {0: load_slab(0, 0, [t_pd_pe])}
            pa = Alloc(nc, OFF_POOL, POOL_SZ, "pf")
            x1b = pa.t("x1", [128, KT * C], F32)
            xin = [pa.t(f"x{r}", [128, C], F32) for r in range(2)]
            vin = [pa.t(f"v{r}", [128, C], F32) for r in range(2)]
            pkin = [pa.t(f"pk{r}", [128, C], F32) for r in range(2)]
            att = [pa.t(f"at{r}", [128, C], F32) for r in range(2)]
            h2o = [pa.t(f"h2{r}", [128, C], BF16) for r in range(2)]
            sq2 = [pa.t(f"sq{r}", [128, C], BF16) for r in range(2)]
            rows = {n_: pa.t(n_, [1, C], F32) for n_ in ("mu", "ms", "var", "rr", "sr")}
            at_t, x1_t, h2_t, x1_sp, h2_sp, sq_rd = {}, {}, {}, {}, {}, {}
            ldx, ldv, ldpk = {}, {}, {}
            prev_ln_reads = None
            prev_stats_read = None
            for n in range(NCH):
                for mi in range(MT):
                    idx = n * MT + mi
                    if idx == 0:
                        guard("sync")
                        w("sync", pd_v_sp.get(NCH * MT - 2))
                        w("sync", pd_v_sp.get(NCH * MT - 1))
                        w("sync", pd_pk_sp.get(NCH * MT - 2))
                        w("sync", pd_pk_sp.get(NCH * MT - 1))
                    w("sync", at_t.get(idx - 2))
                    w("sync", x1_t.get(idx - 2))
                    ldx[idx] = op("sync", lambda e, b=xin[mi % 2], m=mi, nn=n:
                                  e.dma_start(out=b[:], in_=xT[m * 128:(m + 1) * 128,
                                                              nn * C:(nn + 1) * C]),
                                  sLX[mi % 2], 16)
                    ldv[idx] = op("sync", lambda e, b=vin[mi % 2], m=mi, nn=n:
                                  e.dma_start(out=b[:], in_=v_d[m * 128:(m + 1) * 128,
                                                              nn * C:(nn + 1) * C]),
                                  sLV[mi % 2], 16)
                    ldpk[idx] = op("sync", lambda e, b=pkin[mi % 2], m=mi, nn=n:
                                   e.dma_start(out=b[:], in_=pk_d[m * 128:(m + 1) * 128,
                                                                  nn * C:(nn + 1) * C]),
                                   sLPK[mi % 2], 16)
                    w("vector", ldpk[idx])
                    w("vector", ldv[idx])
                    if idx == 0:
                        w("vector", t_pkprev)
                    w("vector", h2_t.get(idx - 2))
                    t_at = op("vector", lambda e, p=pkin[mi % 2], v_=vin[mi % 2],
                              o=att[mi % 2], m=mi:
                              e.scalar_tensor_tensor(o[:], p[:], prevU[3][:, m:m + 1], v_[:],
                                                     ALU.add, ALU.mult), sVe)
                    at_t[idx] = t_at
                    w("vector", ldx[idx])
                    if n > 0 and mi == 0:
                        w("vector", x1_sp.get(n - 1))
                        w("vector", prev_stats_read)
                    t_x1 = op("vector", lambda e, a=att[mi % 2], x=xin[mi % 2], m=mi:
                              e.tensor_add(x1b[:, m * C:(m + 1) * C], a[:], x[:]), sVe)
                    x1_t[idx] = t_x1
                w("sync", x1_t[n * MT + MT - 1])
                w("sync", x1_sp.get(n - 1))
                dstx = split3(x1_d[:, n * C:(n + 1) * C], KT)
                x1_sp[n] = op("sync", lambda e, d=dstx, s_=sb3(x1b, KT, C):
                              e.dma_start(out=d, in_=s_), sX1SP, 16)
                w("tensor", prev_ln_reads)
                for mi in range(MT):
                    w("scalar", last2(sq_rd, (n, mi - 2), (n - 1, mi + MT - 2)))
                    w("scalar", x1_t[n * MT + mi])
                    t_sq = op("scalar", lambda e, m=mi, r=sq2[mi % 2]:
                              e.activation(r[:], x1b[:, m * C:(m + 1) * C], AF.Square,
                                           bias=0.0, scale=1.0), sSc)
                    w("tensor", x1_t[n * MT + mi])
                    op("tensor", lambda e, m=mi: e.matmul(
                        ps[2][0:1, 0:C], ones_cf[:, :], x1b[:, m * C:(m + 1) * C],
                        start=(m == 0), stop=(m == MT - 1)), sPe)
                    w("tensor", t_sq)
                    t_stats = op("tensor", lambda e, m=mi, r=sq2[mi % 2]:
                                 e.matmul(ps[3][0:1, 0:C], ones_cb[:, :], r[:],
                                          start=(m == 0), stop=(m == MT - 1)), sPe)
                    sq_rd[(n, mi)] = t_stats
                prev_stats_read = t_stats
                t_bc = ln_rows_and_bcast(rows, True, t_stats, prev_ln_reads)
                for mi in range(MT):
                    idx = n * MT + mi
                    w("vector", t_bc)
                    w("vector", h2_sp.get(idx - 2))
                    op("vector", lambda e, m=mi, r=att[mi % 2]:
                       e.tensor_mul(r[:], x1b[:, m * C:(m + 1) * C], ps[4][:, 0:C]), sVe)
                    t_h2 = op("vector", lambda e, a=att[mi % 2], o=h2o[mi % 2]:
                              e.tensor_sub(o[:], a[:], ps[5][:, 0:C]), sVe)
                    h2_t[idx] = t_h2
                    w("sync", t_h2)
                    h2_sp[idx] = op("sync", lambda e, o=h2o[mi % 2], m=mi, nn=n:
                                    e.dma_start(out=h2_d[m * 128:(m + 1) * 128,
                                                         nn * C:(nn + 1) * C], in_=o[:]),
                                    sH2SP[mi % 2], 16)
                prev_ln_reads = tokVe()
            pf_x1_sp = dict(x1_sp)
            pf_h2_sp = dict(h2_sp)

        sH2L = [NS("sH2L0"), NS("sH2L1")]
        sHIDSP = [NS("sHIDSP0"), NS("sHIDSP1")]
        if upto >= 7:
            pa = Alloc(nc, OFF_POOL, POOL_SZ, "pg")
            h2b = [pa.t(f"h2{r}", [128, KT * C], BF16) for r in range(2)]
            sgt = [pa.t(f"sg{r}", [128, C], F32) for r in range(2)]
            hio = [pa.t(f"hi{r}", [128, C], BF16) for r in range(2)]
            hid_sp, mm6, sg_t, hi_t = {}, {}, {}, {}
            first_load = True
            for s_ in range(NSLAB):
                reg = s_ % 2
                if s_ + 1 < NSLAB:
                    if s_ == 0:
                        t_slab[1] = load_slab(1, 1, [t_pc_pe])
                    else:
                        t_slab[s_ + 1] = load_slab(
                            s_ + 1, (s_ + 1) % 2,
                            [mm6.get((s_ - 1, NCH - 1, SMT - 1))])
                for n in range(NCH):
                    if first_load:
                        guard("sync")
                        w("sync", pf_h2_sp.get(NCH * MT - 2))
                        w("sync", pf_h2_sp.get(NCH * MT - 1))
                        first_load = False
                    key2 = (s_, n - 2, SMT - 1) if n >= 2 else (s_ - 1, NCH - 2 + n, SMT - 1)
                    w("sync", mm6.get(key2))
                    srch = split3(h2_d[:, n * C:(n + 1) * C], KT)
                    t_h2l = op("sync", lambda e, sc_=srch, b=sb3(h2b[n % 2], KT, C):
                               e.dma_start(out=b, in_=sc_), sH2L[n % 2], 16)
                    for mi in range(SMT):
                        j = n * SMT + mi
                        gbank = ps[mi % 2]
                        ubank = ps[2 + mi % 2]
                        w("tensor", t_h2l)
                        if n == 0 and mi == 0:
                            w("tensor", t_slab[s_])
                        w("tensor", sg_t.get((s_, j - 2)))
                        w("tensor", hi_t.get((s_, j - 2)))
                        for ki in range(KT):
                            t_g = op("tensor", lambda e, k=ki, m=mi, b=gbank, r=h2b[n % 2],
                                     rr=reg:
                                     e.matmul(b[:], pgW[rr][0][:, k * 1024 + m * 128:
                                                               k * 1024 + (m + 1) * 128],
                                              r[:, k * C:(k + 1) * C],
                                              start=(k == 0), stop=(k == KT - 1)),
                                     sPe if ki == KT - 1 else None)
                        for ki in range(KT):
                            tokm = op("tensor", lambda e, k=ki, m=mi, b=ubank, r=h2b[n % 2],
                                      rr=reg:
                                      e.matmul(b[:], pgW[rr][1][:, k * 1024 + m * 128:
                                                                k * 1024 + (m + 1) * 128],
                                               r[:, k * C:(k + 1) * C],
                                               start=(k == 0), stop=(k == KT - 1)),
                                      sPe if ki == KT - 1 else None)
                        mm6[(s_, n, mi)] = tokm
                        w("scalar", t_g)
                        w("scalar", hi_t.get((s_, j - 2)))
                        t_sg = op("scalar", lambda e, b=gbank, o=sgt[mi % 2]:
                                  e.activation(o[:], b[:], AF.Silu, bias=0.0, scale=1.0), sSc)
                        sg_t[(s_, j)] = t_sg
                        w("vector", t_sg)
                        w("vector", tokm)
                        w("vector", hid_sp.get((s_, j - 2)))
                        t_hi = op("vector", lambda e, sg=sgt[mi % 2], b=ubank, o=hio[mi % 2]:
                                  e.tensor_mul(o[:], sg[:], b[:]), sVe)
                        hi_t[(s_, j)] = t_hi
                        w("sync", t_hi)
                        hid_sp[(s_, j)] = op(
                            "sync", lambda e, o=hio[mi % 2], ss=s_, m=mi, nn=n:
                            e.dma_start(out=hid_d[ss * 1024 + m * 128:
                                                  ss * 1024 + (m + 1) * 128,
                                                  nn * C:(nn + 1) * C], in_=o[:]),
                            sHIDSP[mi % 2], 16)
            t_pg_pe = tokPe()
            pg_hid_sp = dict(hid_sp)

        sHIB = [NS("sHIBa"), NS("sHIBb")]
        sHIB2 = [NS("sHIB2a"), NS("sHIB2b")]
        sW2L = [NS("sW2L0"), NS("sW2L1")]
        sLX1 = [NS("sLX10"), NS("sLX11")]
        sYSP = [NS("sYSP0"), NS("sYSP1")]
        if upto >= 8:
            def load_mg(mg, reg, pre):
                for tk in pre:
                    w("gpsimd", tk)
                w("gpsimd", t_ag)
                src = split3(w2pp_all[mg * HF:(mg + 1) * HF, :], GT)
                return op("gpsimd", lambda e, s_=src, d=sb3(phW[reg], GT, 256):
                          e.dma_start(out=d, in_=s_), sW2L[reg], 16)

            t_mg = {0: load_mg(0, 0, [t_pg_pe])}
            pa = Alloc(nc, OFF_POOL, POOL_SZ, "ph")
            hibP = [pa.t(f"hibP{r}", [128, 32 * C], BF16) for r in range(2)]
            hibW = [nc.alloc_sbuf_tensor_at(f"hibW{r}", [128, 32 * C], BF16,
                                            offset=[OFF_W0, OFF_W1][r] + 32768)
                    for r in range(2)]
            x1in = [pa.t(f"x1{r}", [128, C], F32) for r in range(2)]
            oo = [pa.t(f"oo{r}", [128, C], F32) for r in range(2)]
            mm7, y_sp, oo_t, ev7, ldx1 = {}, {}, {}, {}, {}
            t_hlA = {}
            t_hlB = {}
            for n in range(NCH):
                if n == 0:
                    guard("sync")
                    w("sync", pg_hid_sp.get((NSLAB - 1, NCH * SMT - 2)))
                    w("sync", pg_hid_sp.get((NSLAB - 1, NCH * SMT - 1)))
                    w("sync", t_pg_pe)
                for nn2 in ([0, 1] if n == 0 else ([n + 1] if n + 1 < NCH else [])):
                    srch = split3(hid_d[:, nn2 * C:(nn2 + 1) * C], GT)
                    w("sync", mm7.get((nn2 - 2, NMG - 1, 1)))
                    t_hlA[nn2] = op("sync", lambda e, s_=srch[:, 0:32, :],
                                    d=sb3(hibP[nn2 % 2], 32, C):
                                    e.dma_start(out=d, in_=s_), sHIB[nn2 % 2], 16)
                    t_hlB[nn2] = op("sync", lambda e, s_=srch[:, 32:64, :],
                                    d=sb3(hibW[nn2 % 2], 32, C):
                                    e.dma_start(out=d, in_=s_), sHIB2[nn2 % 2], 16)
                t_hl1 = t_hlA[n]
                t_hl = t_hlB[n]
                for mg in range(NMG):
                    reg = mg % 2
                    gidx = n * NMG + mg
                    if mg + 1 < NMG or n + 1 < NCH:
                        nxt = mg + 1 if mg + 1 < NMG else 0
                        prev_key = (n, mg - 1, 1) if mg >= 1 else (n - 1, NMG - 1, 1)
                        t_mg[gidx + 1] = load_mg(nxt, (mg + 1) % 2,
                                                 [mm7.get(prev_key) or t_pg_pe])
                    w("tensor", t_mg[gidx])
                    if mg == 0:
                        w("tensor", t_hl1)
                    else:
                        w("tensor", t_hl)
                    w("tensor", ev7.get(gidx - 4))
                    for ml in range(2):
                        bank = ps[(mg % 4) * 2 + ml]
                        for ki in range(GT):
                            if mg == 0 and ml == 0 and ki == 32:
                                w("tensor", t_hl)
                            hsrc = hibP[n % 2] if ki < 32 else hibW[n % 2]
                            ko_ = ki if ki < 32 else ki - 32
                            tokm = op("tensor", lambda e, k=ki, k2=ko_, m=ml, b=bank, r=reg,
                                      hs=hsrc:
                                      e.matmul(b[:], phW[r][:, k * 256 + m * 128:
                                                            k * 256 + (m + 1) * 128],
                                               hs[:, k2 * C:(k2 + 1) * C],
                                               start=(k == 0), stop=(k == GT - 1)),
                                      sPe if ki == GT - 1 else None)
                        mm7[(n, mg, ml)] = tokm
                    for ml in range(2):
                        j = gidx * 2 + ml
                        mi = mg * 2 + ml
                        bank = ps[(mg % 4) * 2 + ml]
                        obuf = oo[j % 2]
                        w("scalar", mm7[(n, mg, ml)])
                        w("scalar", y_sp.get(j - 2))
                        t_ev = op("scalar", lambda e, b=bank, o=obuf:
                                  e.activation(o[:], b[:], AF.Identity, bias=0.0,
                                               scale=1.0), sSc)
                        ev7[gidx] = t_ev
                        if n == 0 and mg == 0 and ml == 0:
                            w("sync", pf_x1_sp.get(NCH - 1))
                        w("sync", oo_t.get(j - 2))
                        ldx1[j] = op("sync", lambda e, b=x1in[ml], m=mi, nn=n:
                                     e.dma_start(out=b[:], in_=x1_d[m * 128:(m + 1) * 128,
                                                                    nn * C:(nn + 1) * C]),
                                     sLX1[ml], 16)
                        w("vector", t_ev)
                        w("vector", ldx1[j])
                        t_y = op("vector", lambda e, o=obuf, x=x1in[ml]:
                                 e.tensor_add(o[:], o[:], x[:]), sVe)
                        oo_t[j] = t_y
                        w("gpsimd", t_y)
                        w("gpsimd", y_sp.get(j - 2))
                        y_sp[j] = op("gpsimd", lambda e, o=obuf, m=mi, nn=n:
                                     e.dma_start(out=y[m * 128:(m + 1) * 128,
                                                       nn * C:(nn + 1) * C], in_=o[:]),
                                     sYSP[j % 2], 16)
            w("gpsimd", y_sp[NCH * NMG * 2 - 2])
            w("gpsimd", y_sp[NCH * NMG * 2 - 1])

        if taps:
            guard("gpsimd")
            for tname in taps:
                src = scr_map[tname]
                dst = tap_outs[tname]
                t_t = op("gpsimd", lambda e, s_=src.tensor.ap(), d=dst.tensor.ap():
                         e.dma_start(out=d, in_=s_), sCcD, 16)
            w("gpsimd", t_t)

        @block.tensor
        def _(t):
            for f in streams["tensor"]:
                f(t)

        @block.vector
        def _(v):
            for f in streams["vector"]:
                f(v)

        @block.scalar
        def _(s):
            for f in streams["scalar"]:
                f(s)

        @block.sync
        def _(s):
            for f in streams["sync"]:
                f(s)

        @block.gpsimd
        def _(g):
            for f in streams["gpsimd"]:
                f(g)

    return nc


def _hadamard128():
    h = np.array([[1.0]], dtype=np.float32)
    while h.shape[0] < 128:
        h = np.block([[h, h], [h, -h]])
    return h.astype(np.float32)


def make_in_maps(inputs):
    x = np.asarray(inputs["L0_attn_hidden_states"], np.float32)
    WqT = np.ascontiguousarray(np.asarray(inputs["L0_attn_Wq"], np.float32).T)
    WqaT = np.ascontiguousarray(np.asarray(inputs["L0_attn_Wqa"], np.float32).T)
    WkT = np.ascontiguousarray(np.asarray(inputs["L0_attn_Wk"], np.float32).T)
    WkaT = np.ascontiguousarray(np.asarray(inputs["L0_attn_Wka"], np.float32).T)
    WvT = np.ascontiguousarray(np.asarray(inputs["L0_attn_Wv"], np.float32).T)
    W1T = np.ascontiguousarray(np.asarray(inputs["L0_ffn_W1"], np.float32).T)
    W3T = np.ascontiguousarray(np.asarray(inputs["L0_ffn_W3"], np.float32).T)
    W2T = np.ascontiguousarray(np.asarray(inputs["L0_ffn_W2"], np.float32).T)
    h128 = _hadamard128()

    def col16(v):
        return np.ascontiguousarray(np.asarray(v, np.float32).reshape(16, 128).T)

    bq = col16(inputs["L0_attn_bq"]); bqa = col16(inputs["L0_attn_bqa"])
    bk = col16(inputs["L0_attn_bk"]); bka = col16(inputs["L0_attn_bka"])
    bv = col16(inputs["L0_attn_bv"])
    lng = col16(inputs["L0_attn_ln_g"]); lnb = col16(inputs["L0_attn_ln_b"])

    in_maps = []
    for c in range(8):
        b, hf = c >> 1, c & 1
        xTc = np.ascontiguousarray(x[b, hf * T:(hf + 1) * T, :].T)
        w2sc = np.ascontiguousarray(W2T[:, c * 256:(c + 1) * 256])
        cc = np.zeros((128, 114), np.float32)
        cc[:, 0:16] = bq; cc[:, 16:32] = bqa; cc[:, 32:48] = bk
        cc[:, 48:64] = bka; cc[:, 64:80] = bv
        cc[:, 80:96] = lng; cc[:, 96:112] = lnb
        cc[:, 112] = 1.0 - hf
        cc[:, 113] = float(hf)
        in_maps.append({
            "xT": xTc, "wq": WqT, "wqa": WqaT, "wk": WkT, "wka": WkaT, "wv": WvT,
            "w1": W1T, "w3": W3T, "w2s": w2sc, "consts": cc, "h128": h128,
        })
    return in_maps


def kernel(**inputs):
    nc = build_program(upto=UPTO, taps=TAPS)
    in_maps = make_in_maps(inputs)
    res = run_bass_kernel_spmd(nc, in_maps, core_ids=list(range(8)))
    out = np.empty((B, S, D), np.float32)
    for c in range(8):
        b, hf = c >> 1, c & 1
        out[b, hf * T:(hf + 1) * T, :] = res.results[c]["y"].T
    if TAPS:
        kernel.tap_results = res.results
    return out


# version 10
# speedup vs baseline: 1.0168x; 1.0168x over previous
import numpy as np
from contextlib import ExitStack

import concourse.bass as bass
import concourse.mybir as mybir
from concourse.bass_utils import run_bass_kernel_spmd

F32 = mybir.dt.float32
BF16 = mybir.dt.bfloat16
AF = mybir.ActivationFunctionType
ALU = mybir.AluOpType

B, S, D, HF = 4, 8192, 2048, 8192
T = 4096
C = 512
NCH = T // C
KT = D // 128
MT = D // 128
GT = HF // 128
EPS = 1e-5
SCALE = float(D) ** -0.5
FWS = float(HF) ** -0.5
NSLAB = 8
SMT = 8
NMG = 8

SB_BASE = 16512
OFF_W0 = SB_BASE
OFF_W1 = SB_BASE + 65536
OFF_POOL = SB_BASE + 131072
POOL_SZ = 73728
OFF_CONST = SB_BASE + 204800
CONST_SZ = 8032

UPTO = 99
TAPS = ()


class Sm:
    def __init__(self, h):
        self.h = h
        self.n = 0


class Alloc:
    def __init__(self, nc, off, size, tag):
        self.nc, self.off, self.end = nc, off, off + size
        self.tag = tag
        self.i = 0

    def t(self, name, shape, dtype):
        bpp = int(np.prod(shape[1:])) * (2 if dtype == BF16 else 4)
        bpp = (bpp + 63) // 64 * 64
        assert self.off + bpp <= self.end, (
            f"{self.tag}: {name} needs {bpp}B at {self.off}, end {self.end}")
        h = self.nc.alloc_sbuf_tensor_at(f"{self.tag}_{name}_{self.i}", list(shape),
                                         dtype, offset=self.off)
        self.i += 1
        self.off += bpp
        return h


def build_program(upto=99, taps=()):
    nc = bass.Bass("TRN2", target_bir_lowering=False)

    xT = nc.dram_tensor("xT", [D, T], F32, kind="ExternalInput").ap()
    wq = nc.dram_tensor("wq", [D, D], F32, kind="ExternalInput").ap()
    wqa = nc.dram_tensor("wqa", [D, D], F32, kind="ExternalInput").ap()
    wk = nc.dram_tensor("wk", [D, D], F32, kind="ExternalInput").ap()
    wka = nc.dram_tensor("wka", [D, D], F32, kind="ExternalInput").ap()
    wv = nc.dram_tensor("wv", [D, D], F32, kind="ExternalInput").ap()
    w1 = nc.dram_tensor("w1", [D, HF], F32, kind="ExternalInput").ap()
    w3 = nc.dram_tensor("w3", [D, HF], F32, kind="ExternalInput").ap()
    w2s = nc.dram_tensor("w2s", [HF, 256], F32, kind="ExternalInput").ap()
    consts = nc.dram_tensor("consts", [128, 114], F32, kind="ExternalInput").ap()
    h128 = nc.dram_tensor("h128", [128, 128], F32, kind="ExternalInput").ap()
    y = nc.dram_tensor("y", [D, T], F32, kind="ExternalOutput").ap()

    def split3(dram2d, kk):
        return dram2d.rearrange("(k p) c -> p k c", p=128)

    def sb3(sbt, kk, cc):
        return sbt[:, :].rearrange("p (k c) -> p k c", k=kk)

    def scratch(name, shape, dtype):
        return nc.dram_tensor(name, shape, dtype, kind="Internal").ap()

    q_d = scratch("q_d", [D, T], F32)
    arg_d = scratch("arg_d", [D, T], F32)
    q1l_d = scratch("q1l_d", [D, T], F32)
    k_d = scratch("k_d", [D, T], F32)
    p_d = scratch("p_d", [D, T], F32)
    mk_d = scratch("mk_d", [D, T], F32)
    barg_d = scratch("barg_d", [D, T], F32)
    k1l_d = scratch("k1l_d", [D, T], F32)
    v_d = scratch("v_d", [D, T], F32)
    pk_d = scratch("pk_d", [D, T], F32)
    x1_d = scratch("x1_d", [D, T], F32)
    h_d = scratch("h_d", [D, T], BF16)
    h2_d = scratch("h2_d", [D, T], BF16)
    hid_d = scratch("hid_d", [HF, T], BF16)
    w2pp_all = scratch("w2pp_all", [NMG * HF, 256], BF16)
    cc_in = [scratch(f"cc_in{i}", [128, 16], F32) for i in range(4)]
    cc_out = [scratch(f"cc_out{i}", [128, 16], F32) for i in range(4)]
    w2pp_mine = scratch("w2pp_mine", [HF, 256], BF16)
    scr_map = {"q_d": q_d, "arg_d": arg_d, "q1l_d": q1l_d, "k_d": k_d,
               "p_d": p_d, "mk_d": mk_d, "barg_d": barg_d, "k1l_d": k1l_d,
               "v_d": v_d, "pk_d": pk_d, "x1_d": x1_d, "h_d": h_d,
               "h2_d": h2_d, "hid_d": hid_d}
    tap_outs = {}
    for tname in taps:
        src = scr_map[tname]
        tap_outs[tname] = nc.dram_tensor(
            "tap_" + tname, list(src.tensor.shape), src.tensor.dtype,
            kind="ExternalOutput").ap()

    wbuf0 = nc.alloc_sbuf_tensor_at("wbuf0", [128, KT * D], BF16, offset=OFF_W0)
    wbuf1 = nc.alloc_sbuf_tensor_at("wbuf1", [128, KT * D], BF16, offset=OFF_W1)
    p0A = nc.alloc_sbuf_tensor_at("p0A", [128, 16384], F32, offset=OFF_W0)
    p0B = nc.alloc_sbuf_tensor_at("p0B", [128, 16384], F32, offset=OFF_W1)
    pgW = [(nc.alloc_sbuf_tensor_at(f"pgW1_{r}", [128, KT * 1024], BF16,
                                    offset=[OFF_W0, OFF_W1][r]),
            nc.alloc_sbuf_tensor_at(f"pgW3_{r}", [128, KT * 1024], BF16,
                                    offset=[OFF_W0, OFF_W1][r] + 32768))
           for r in range(2)]
    phW = [nc.alloc_sbuf_tensor_at(f"phW_{r}", [128, GT * 256], BF16,
                                   offset=[OFF_W0, OFF_W1][r]) for r in range(2)]

    ca = Alloc(nc, OFF_CONST, CONST_SZ, "cst")
    cst = ca.t("consts", [128, 114], F32)
    bqa_s = ca.t("bqa_s", [128, 16], F32)
    bka_s = ca.t("bka_s", [128, 16], F32)
    Q1c = ca.t("Q1c", [128, 16], F32)
    Pc = ca.t("Pc", [128, 16], F32)
    K1c = ca.t("K1c", [128, 16], F32)
    PKc = ca.t("PKc", [128, 16], F32)
    prevU = [ca.t(f"prevU{i}", [128, 16], F32) for i in range(4)]
    ccsnd = ca.t("ccsnd", [128, 16], F32)
    ccraw = ca.t("ccraw", [128, 16], F32)
    zeros = ca.t("zeros", [128, C], F32)
    ones_cb = ca.t("ones_cb", [128, 1], BF16)
    ones_cf = ca.t("ones_cf", [128, 1], F32)
    ones_r = ca.t("ones_r", [1, 128], F32)
    h128b = ca.t("h128b", [128, 128], BF16)

    def bias_col(idx, mi):
        return cst[:, idx * 16 + mi:idx * 16 + mi + 1]
    sendm = cst[:, 112:113]
    usem = cst[:, 113:114]

    ps = [nc.alloc_psum_tensor(f"ps{i}", [128, C], F32) for i in range(8)]

    streams = {k: [] for k in ("tensor", "vector", "scalar", "sync", "gpsimd")}

    def em(eng, f):
        streams[eng].append(f)

    def op(eng, f, sem=None, k=1):
        if sem is None:
            em(eng, f)
            return None
        sem.n += k
        tok = (sem, sem.n)
        em(eng, lambda e, f=f, h=sem.h, kk=k: f(e).then_inc(h, kk))
        return tok

    def w(eng, tok):
        if tok is None:
            return
        sem, v = tok
        if v <= 0:
            return
        em(eng, lambda e, h=sem.h, t=v: e.wait_ge(h, t))

    def last2(dct, key_new, key_old):
        return dct.get(key_new) or dct.get(key_old)

    with ExitStack() as st:
        def NS(name):
            return Sm(st.enter_context(nc.semaphore(name)))

        sPe, sSc, sVe, sCa, sCc = NS("sPe"), NS("sSc"), NS("sVe"), NS("sCa"), NS("sCc")
        sGp = NS("sGp")

        def tokPe():
            return (sPe, sPe.n)

        def tokVe():
            return (sVe, sVe.n)

        def tokSc():
            return (sSc, sSc.n)

        def tokCa():
            return (sCa, sCa.n)

        def tokGp():
            return (sGp, sGp.n)

        block = st.enter_context(nc.Block())

        def guard(eng):
            for tk in (tokPe(), tokVe(), tokSc(), tokCa(), tokGp()):
                w(eng, tk)

        sIni = NS("sIni")
        t_cst = op("sync", lambda e: e.dma_start(out=cst[:, :], in_=consts[:, :]), sIni, 16)
        op("gpsimd", lambda e: e.dma_start(out=h128b[:], in_=h128[:, :]), sIni, 16)
        for buf, val in ((zeros, 0.0), (Q1c, 0.0), (Pc, 0.0), (K1c, 0.0), (PKc, 0.0),
                         (ones_cb, 1.0), (ones_cf, 1.0), (ones_r, 1.0)):
            op("gpsimd", lambda e, b=buf, v=val: e.memset(b[:], v), sIni, 1)
        t_init = (sIni, sIni.n)
        w("vector", t_cst)
        op("vector", lambda e: e.tensor_scalar_mul(bqa_s[:], cst[:, 16:32], SCALE), sVe)
        op("vector", lambda e: e.tensor_scalar_mul(bka_s[:], cst[:, 48:64], SCALE), sVe)
        t_bias_ready = tokVe()

        w2mid_d = scratch("w2mid_d", [NMG, HF, 256], BF16)[0]
        sAL = [NS("sAL0")]
        sAM = [NS("sAM0"), NS("sAM1")]
        sBL = [NS("sBL0"), NS("sBL1")]
        sP0W = [NS("sP0W0"), NS("sP0W1")]
        sGat = NS("sGat")
        t_w2pp_done = []
        t_p0_pe = (sPe, 0)
        t_ag = None
        if upto >= 0:
            pa = Alloc(nc, OFF_POOL, POOL_SZ, "p0a")
            slabA = [pa.t(f"slA{r}", [128, GT * 256], BF16) for r in range(2)]
            obA = [pa.t(f"obA{r}", [128, 8 * 256], BF16) for r in range(2)]
            midb = [nc.alloc_sbuf_tensor_at(f"midb{r}", [64, 64 * 256], BF16,
                                            offset=OFF_W0 + r * 32768) for r in range(2)]
            obB = [nc.alloc_sbuf_tensor_at(f"obB{r}", [64, 8 * 512], BF16,
                                           offset=OFF_W1 + r * 8192) for r in range(2)]
            mmA, evA, stA = {}, {}, {}
            mmB, evB, stB = {}, {}, {}
            srcap = split3(w2s, GT)
            w("gpsimd", t_init)
            t_ld = op("gpsimd", lambda e, s_=srcap, d=sb3(slabA[0], GT, 256):
                      e.dma_start(out=d, in_=s_), sAL[0], 16)
            for gi in range(GT):
                grp = gi // 8
                bank = ps[gi % 2]
                w("tensor", t_ld)
                if gi == 0:
                    w("tensor", t_init)
                w("tensor", evA.get(gi - 2))
                t_mm = op("tensor", lambda e, b=bank, g=gi, sl=slabA[0]:
                          e.matmul(b[:, 0:256], h128b[:], sl[:, g * 256:(g + 1) * 256],
                                   start=True, stop=True), sPe)
                mmA[gi] = t_mm
                w("scalar", t_mm)
                if gi % 8 == 0:
                    w("scalar", stA.get(grp - 2))
                t_ev = op("scalar", lambda e, o=obA[(gi // 8) % 2], b=bank, g=gi:
                          e.activation(o[:, (g % 8) * 256:(g % 8 + 1) * 256], b[:, 0:256],
                                       AF.Identity, bias=0.0, scale=FWS), sSc)
                evA[gi] = t_ev
                if gi % 8 == 7:
                    w("sync", t_ev)
                    g0 = gi - 7
                    dstm = split3(w2mid_d[g0 * 128:(g0 + 8) * 128, :], 8)
                    stA[grp] = op("sync", lambda e, o=sb3(obA[(gi // 8) % 2], 8, 256),
                                  d=dstm: e.dma_start(out=d, in_=o), sAM[(gi // 8) % 2], 16)
            for hh in range(2):
                srcb = w2mid_d.rearrange("(g c) d -> g c d", c=128)[
                    :, hh * 64:(hh + 1) * 64, :]
                w("sync", stA.get(6))
                w("sync", stA.get(7))
                t_bl = op("sync", lambda e, s_=srcb,
                          d=midb[hh][0:64, :].rearrange("p (c d) -> p c d", c=64):
                          e.dma_start(out=d, in_=s_), sBL[hh], 16)
                for jj in range(32):
                    gj = hh * 32 + jj
                    grpb = gj // 8
                    bank = ps[2 + jj % 2]
                    w("tensor", t_bl)
                    w("tensor", evB.get(gj - 2))
                    t_mm = op("tensor", lambda e, b=bank, m=midb[hh], j2=jj:
                              e.matmul(b[0:64, :], h128b[0:64, 0:64],
                                       m[0:64, j2 * 512:(j2 + 1) * 512],
                                       start=True, stop=True), sPe)
                    mmB[(hh, jj)] = t_mm
                    w("vector", t_mm)
                    if gj % 8 == 0:
                        w("vector", stB.get(grpb - 2))
                    t_ev = op("vector", lambda e, o=obB[grpb % 2], b=bank, j2=jj:
                              e.tensor_copy(o[0:64, (j2 % 8) * 512:(j2 % 8 + 1) * 512],
                                            b[0:64, :]), sVe)
                    evB[gj] = t_ev
                    if gj % 8 == 7:
                        w("gpsimd", t_ev)
                        c0 = hh * 64 + (jj - 7) * 2
                        dstap = w2pp_mine.rearrange("(g c) d -> g c d", c=128)[
                            :, c0:c0 + 16, :]
                        stB[grpb] = op("gpsimd", lambda e, o=obB[grpb % 2][0:64, :]
                                       .rearrange("g (c d) -> g c d", c=16),
                                       d=dstap: e.dma_start(out=d, in_=o),
                                       sP0W[grpb % 2], 16)
            t_w2pp_done = [(sP0W[0], sP0W[0].n), (sP0W[1], sP0W[1].n)]
            t_p0_pe = tokPe()

        sWq, sWqa, sWk, sWka, sWv = NS("sWq"), NS("sWqa"), NS("sWk"), NS("sWka"), NS("sWv")

        def load_w2048(dst, srcap, sem, pre):
            for tk in pre:
                w("gpsimd", tk)
            src = split3(srcap, KT)
            return op("gpsimd", lambda e, d=sb3(dst, KT, D), s=src:
                      e.dma_start(out=d, in_=s), sem, 16)

        if upto >= 1:
            t_wq = load_w2048(wbuf0, wq, sWq, [t_p0_pe])
            t_wqa = load_w2048(wbuf1, wqa, sWqa,
                               list(t_w2pp_done))

        def ln_rows_and_bcast(rw, plain, t_stats_pe, prev_bcast_reads):
            mu, ms, var, rr, sr = rw["mu"], rw["ms"], rw["var"], rw["rr"], rw["sr"]
            w("vector", t_stats_pe)
            op("vector", lambda e, a=mu: e.tensor_scalar_mul(a[:], ps[2][0:1, 0:C], 1.0 / D), sVe)
            op("vector", lambda e, a=ms: e.tensor_scalar_mul(a[:], ps[3][0:1, 0:C], 1.0 / D), sVe)
            op("vector", lambda e, a=var, b=mu: e.tensor_mul(a[:], b[:], b[:]), sVe)
            op("vector", lambda e, a=var, b=ms: e.tensor_sub(a[:], b[:], a[:]), sVe)
            if plain:
                t_t = op("vector", lambda e, a=var: e.tensor_scalar_add(a[:], a[:], EPS), sVe)
            else:
                t_t = op("vector", lambda e, a=var: e.tensor_scalar(
                    a[:], a[:], 1.0 + EPS, EPS * EPS, ALU.mult, ALU.add), sVe)
            w("scalar", t_t)
            t_ln = op("scalar", lambda e, a=rr, b=var:
                      e.activation(a[:], b[:], AF.Ln, bias=0.0, scale=1.0), sSc)
            w("scalar", t_ln)
            t_rr = op("scalar", lambda e, a=rr:
                      e.activation(a[:], a[:], AF.Exp, bias=0.0, scale=-0.5), sSc)
            w("vector", t_rr)
            t_sr = op("vector", lambda e, a=sr, b=mu, c=rr: e.tensor_mul(a[:], b[:], c[:]), sVe)
            w("tensor", t_sr)
            w("tensor", prev_bcast_reads)
            op("tensor", lambda e, a=rr: e.matmul(ps[4][:, 0:C], ones_r[:, :], a[:, :],
                                                  start=True, stop=True), sPe)
            t_bc = op("tensor", lambda e, a=sr: e.matmul(ps[5][:, 0:C], ones_r[:, :], a[:, :],
                                                         start=True, stop=True), sPe)
            return t_bc

        pa = Alloc(nc, OFF_POOL, POOL_SZ, "pa")
        xbf = pa.t("xbf", [128, KT * C], BF16)
        hb = [pa.t(f"h{r}", [128, KT * C], BF16) for r in range(2)]
        qo = [pa.t(f"qo{r}", [128, C], F32) for r in range(2)]
        sqt = [pa.t(f"sq{r}", [128, C], BF16) for r in range(2)]
        t1b = [pa.t(f"t1{r}", [128, C], F32) for r in range(2)]
        t2b = [pa.t(f"t2{r}", [128, C], F32) for r in range(2)]
        rows = {n_: pa.t(n_, [1, C], F32) for n_ in ("mu", "ms", "var", "rr", "sr")}
        sXL = NS("sXL")
        sHSP = [NS("sHSP0"), NS("sHSP1")]
        sQSP = [NS("sQSP0"), NS("sQSP1")]
        if upto >= 1:
            h_done = {}
            t_pa_mm = {}
            h_sp = {}
            q_sp = {}
            ev_q = {}
            sq_rd = {}
            prev_xbf_users = None
            prev_stats_read = None
            prev_ln_reads = None
            for n in range(NCH):
                src = split3(xT[:, n * C:(n + 1) * C], KT)
                if n == 0:
                    w("gpsimd", t_init)
                    w("gpsimd", t_p0_pe)
                    w("vector", t_w2pp_done[0] if t_w2pp_done else None)
                    w("vector", t_w2pp_done[1] if t_w2pp_done else None)
                else:
                    w("gpsimd", prev_xbf_users)
                    w("gpsimd", prev_stats_read)
                t_x = op("gpsimd", lambda e, s=src, d=sb3(xbf, KT, C):
                         e.dma_start(out=d, in_=s), sXL, 16)
                if n == 0:
                    w("scalar", t_w2pp_done[0])
                    w("scalar", t_w2pp_done[1])
                w("scalar", t_x)
                w("tensor", t_x)
                if n > 0:
                    w("tensor", prev_ln_reads)
                for mi in range(MT):
                    w("scalar", last2(sq_rd, (n, mi - 2), (n - 1, mi + MT - 2)))
                    t_sq = op("scalar", lambda e, m=mi, r=sqt[mi % 2]:
                              e.activation(r[:], xbf[:, m * C:(m + 1) * C], AF.Square,
                                           bias=0.0, scale=1.0), sSc)
                    op("tensor", lambda e, m=mi: e.matmul(
                        ps[2][0:1, 0:C], ones_cb[:, :], xbf[:, m * C:(m + 1) * C],
                        start=(m == 0), stop=(m == MT - 1)), sPe)
                    w("tensor", t_sq)
                    t_stats = op("tensor", lambda e, m=mi, r=sqt[mi % 2]: e.matmul(
                        ps[3][0:1, 0:C], ones_cb[:, :], r[:],
                        start=(m == 0), stop=(m == MT - 1)), sPe)
                    sq_rd[(n, mi)] = t_stats
                prev_stats_read = t_stats
                t_bc = ln_rows_and_bcast(rows, False, t_stats, prev_ln_reads)
                hcur = hb[n % 2]
                for mi in range(MT):
                    w("vector", t_bc)
                    if n >= 2:
                        w("vector", t_pa_mm.get((n - 2, MT - 1)))
                        w("vector", h_sp.get(n - 2))
                    op("vector", lambda e, m=mi, r=t1b[mi % 2]:
                       e.tensor_mul(r[:], xbf[:, m * C:(m + 1) * C], ps[4][:, 0:C]), sVe)
                    t_t2 = op("vector", lambda e, a=t1b[mi % 2], b=t2b[mi % 2]:
                              e.tensor_sub(b[:], a[:], ps[5][:, 0:C]), sVe)
                    w("scalar", t_t2)
                    t_h = op("scalar", lambda e, m=mi, b=t2b[mi % 2], hc=hcur:
                             e.activation(hc[:, m * C:(m + 1) * C], b[:], AF.Identity,
                                          bias=bias_col(6, m), scale=bias_col(5, m)), sSc)
                    h_done[(n, mi)] = t_h
                prev_xbf_users = tokVe()
                prev_ln_reads = tokVe()
                w("sync", h_done[(n, MT - 1)])
                if n >= 2:
                    w("sync", t_pa_mm.get((n - 2, MT - 1)))
                dsth = split3(h_d[:, n * C:(n + 1) * C], KT)
                h_sp[n] = op("sync", lambda e, d=dsth, hc=sb3(hcur, KT, C):
                             e.dma_start(out=d, in_=hc), sHSP[n % 2], 16)
                if n == 0:
                    w("tensor", t_wq)
                for mi in range(MT):
                    idx = n * MT + mi
                    bank = [ps[0], ps[1], ps[6], ps[7]][mi % 4]
                    w("tensor", h_done[(n, MT - 1)])
                    w("tensor", ev_q.get(idx - 4))
                    for ki in range(KT):
                        tok = op("tensor", lambda e, k=ki, m=mi, b=bank, hc=hcur:
                                 e.matmul(b[:], wbuf0[:, k * D + m * 128:k * D + (m + 1) * 128],
                                          hc[:, k * C:(k + 1) * C],
                                          start=(k == 0), stop=(k == KT - 1)),
                                 sPe if ki == KT - 1 else None)
                    t_pa_mm[(n, mi)] = tok
                    w("scalar", tok)
                    w("scalar", q_sp.get(idx - 2))
                    t_ev = op("scalar", lambda e, b=bank, m=mi, r=qo[mi % 2]:
                              e.activation(r[:], b[:], AF.Identity, bias=bias_col(0, m),
                                           scale=1.0), sSc)
                    ev_q[idx] = t_ev
                    w("sync", t_ev)
                    q_sp[idx] = op("sync", lambda e, m=mi, r=qo[mi % 2], nn=n:
                                   e.dma_start(out=q_d[m * 128:(m + 1) * 128,
                                                       nn * C:(nn + 1) * C],
                                               in_=r[:]), sQSP[mi % 2], 16)
            t_pa_pe = tokPe()
            pa_h_sp = dict(h_sp)
            pa_q_sp = dict(q_sp)
            w("gpsimd", t_w2pp_done[0])
            w("gpsimd", t_w2pp_done[1])
            t_ag = op("gpsimd", lambda e: e.collective_compute(
                "AllGather", ALU.bypass,
                replica_groups=[[0, 1, 2, 3, 4, 5, 6, 7]],
                ins=[w2pp_mine.tensor.ap().opt()],
                outs=[w2pp_all.tensor.ap().opt()]), sGat, 1)

        def scan_stage(sid, n, mi, data_sb, carry, out_sb, ct):
            w("vector", ct.get((sid, n - 1, mi)))
            t_s = op("vector", lambda e, d_=data_sb, o=out_sb, cc=carry, m=mi:
                     e.tensor_tensor_scan(o[:], d_[:], zeros[:], cc[:, m:m + 1],
                                          ALU.add, ALU.add), sCa)
            w("vector", t_s)
            t_c = op("vector", lambda e, o=out_sb, cc=carry, m=mi:
                     e.tensor_copy(cc[:, m:m + 1], o[:, C - 1:C]), sCa)
            ct[(sid, n, mi)] = t_c
            return t_s, t_c

        sQBF = [NS("sQBF0"), NS("sQBF1")]
        sASP = [NS("sASP0"), NS("sASP1")]
        sQ1SP = [NS("sQ1SP0"), NS("sQ1SP1")]
        if upto >= 2:
            pa = Alloc(nc, OFF_POOL, POOL_SZ, "pa2")
            qbf = [pa.t(f"qbf{r}", [128, KT * C], BF16) for r in range(2)]
            qwt = [pa.t(f"qwt{r}", [128, C], F32) for r in range(2)]
            q1o = [pa.t(f"q1o{r}", [128, C], F32) for r in range(2)]
            ev_a, a_sp, q1_sp, ct, mm2, scand = {}, {}, {}, {}, {}, {}
            for n in range(NCH):
                src = split3(q_d[:, n * C:(n + 1) * C], KT)
                if n == 0:
                    guard("gpsimd")
                w("gpsimd", pa_q_sp.get(n * MT + MT - 2))
                w("gpsimd", pa_q_sp.get(n * MT + MT - 1))
                w("gpsimd", mm2.get((n - 2, MT - 1)))
                t_q = op("gpsimd", lambda e, s=src, b=sb3(qbf[n % 2], KT, C):
                         e.dma_start(out=b, in_=s), sQBF[n % 2], 16)
                w("tensor", t_q)
                if n == 0:
                    w("tensor", t_wqa)
                for mi in range(MT):
                    idx = n * MT + mi
                    bank = ps[mi % 4]
                    w("tensor", ev_a.get(idx - 4))
                    for ki in range(KT):
                        tokm = op("tensor", lambda e, k=ki, m=mi, b=bank, r=qbf[n % 2]:
                                  e.matmul(b[:], wbuf1[:, k * D + m * 128:k * D + (m + 1) * 128],
                                           r[:, k * C:(k + 1) * C],
                                           start=(k == 0), stop=(k == KT - 1)),
                                  sPe if ki == KT - 1 else None)
                    mm2[(n, mi)] = tokm
                    w("scalar", tokm)
                    if idx == 0:
                        w("scalar", t_bias_ready)
                    w("scalar", a_sp.get(idx - 2))
                    w("scalar", scand.get(idx - 2))
                    t_qw = op("scalar", lambda e, b=bank, m=mi, o=qwt[mi % 2]:
                              e.activation(o[:], b[:], AF.Exp,
                                           bias=bqa_s[:, m:m + 1], scale=SCALE), sSc)
                    ev_a[idx] = t_qw
                    w("sync", t_qw)
                    a_sp[idx] = op("sync", lambda e, o=qwt[mi % 2], m=mi, nn=n:
                                   e.dma_start(out=arg_d[m * 128:(m + 1) * 128,
                                                         nn * C:(nn + 1) * C],
                                               in_=o[:]), sASP[mi % 2], 16)
                    w("vector", t_qw)
                    if idx == 0:
                        w("vector", t_init)
                    w("vector", q1_sp.get(idx - 2))
                    t_s, t_c = scan_stage(0, n, mi, qwt[mi % 2], Q1c, q1o[mi % 2], ct)
                    scand[idx] = t_s
                    w("sync", t_s)
                    q1_sp[idx] = op("sync", lambda e, o=q1o[mi % 2], m=mi, nn=n:
                                    e.dma_start(out=q1l_d[m * 128:(m + 1) * 128,
                                                          nn * C:(nn + 1) * C],
                                                in_=o[:]), sQ1SP[mi % 2], 16)
            t_pa2_carry = ct[(0, NCH - 1, MT - 1)]
            t_pa2_pe = tokPe()
            pa2_a_sp = dict(a_sp)
            pa2_q1_sp = dict(q1_sp)

        sCcD = NS("sCcD")
        cc_prev = [None]

        def pair_exchange(cid, carry_sb, prev_sb, t_carry_final):
            w("vector", t_carry_final)
            w("vector", cc_prev[0])
            t_snd = op("vector", lambda e, c_=carry_sb:
                       e.tensor_scalar_mul(ccsnd[:], c_[:], sendm), sVe)
            w("gpsimd", t_snd)
            t_d1 = op("gpsimd", lambda e, i=cid: e.dma_start(out=cc_in[i][:, :], in_=ccsnd[:]),
                      sCcD, 16)
            cc_prev[0] = t_d1
            w("gpsimd", t_d1)
            t_cc = op("gpsimd", lambda e, i=cid: e.collective_compute(
                "AllReduce", ALU.add,
                replica_groups=[[0, 1], [2, 3], [4, 5], [6, 7]],
                ins=[cc_in[i].tensor.ap().opt()], outs=[cc_out[i].tensor.ap().opt()]), sCc, 1)
            w("gpsimd", t_cc)
            t_d2 = op("gpsimd", lambda e, i=cid: e.dma_start(out=ccraw[:], in_=cc_out[i][:, :]),
                      sCcD, 16)
            w("vector", t_d2)
            return op("vector", lambda e, p=prev_sb:
                      e.tensor_scalar_mul(p[:], ccraw[:], usem), sVe)

        if upto >= 3:
            t_q1prev = pair_exchange(0, Q1c, prevU[0], t_pa2_carry)
            t_wk = load_w2048(wbuf0, wk, sWk, [t_pa_pe])

        sLDa = [NS(f"sLDa{i}") for i in range(4)]
        sLDb = [NS(f"sLDb{i}") for i in range(4)]
        sLDc = [NS(f"sLDc{i}") for i in range(4)]
        sLDd = [NS(f"sLDd{i}") for i in range(4)]
        sLDe = [NS(f"sLDe{i}") for i in range(4)]

        sHL = [NS("sHL0"), NS("sHL1")]
        sLQ, sLA, sLQ1 = sLDa, sLDb, sLDc
        sKSP = [NS("sKSP0"), NS("sKSP1")]
        sPSP = [NS("sPSP0"), NS("sPSP1")]
        if upto >= 3:
            pa = Alloc(nc, OFF_POOL, POOL_SZ, "pb")
            hbuf = [pa.t(f"h{r}", [128, KT * C], BF16) for r in range(2)]
            R4 = 4
            qin = [pa.t(f"q{r}", [128, C], F32) for r in range(R4)]
            ain = [pa.t(f"a{r}", [128, C], F32) for r in range(R4)]
            q1in = [pa.t(f"q1{r}", [128, C], F32) for r in range(R4)]
            ko = [pa.t(f"ko{r}", [128, C], F32) for r in range(2)]
            ev_k, k_sp, p_sp, ct = {}, {}, {}, {}
            mm3, d_t, n1_t, div_t, scand = {}, {}, {}, {}, {}
            ldq, lda, ldq1 = {}, {}, {}

            def spill_p(j):
                jm, jn = j % MT, j // MT
                w("gpsimd", scand[j])
                p_sp[j] = op("gpsimd", lambda e, o=qin[jm % R4], m=jm, nn=jn:
                             e.dma_start(out=p_d[m * 128:(m + 1) * 128,
                                                 nn * C:(nn + 1) * C], in_=o[:]),
                             sPSP[j % 2], 16)

            for n in range(NCH):
                srch = split3(h_d[:, n * C:(n + 1) * C], KT)
                if n == 0:
                    guard("sync")
                    w("sync", pa_h_sp.get(NCH - 2))
                    w("sync", pa_h_sp.get(NCH - 1))
                w("sync", mm3.get((n - 2, MT - 1)))
                t_h = op("sync", lambda e, s=srch, b=sb3(hbuf[n % 2], KT, C):
                         e.dma_start(out=b, in_=s), sHL[n % 2], 16)
                for mi in range(MT):
                    idx = n * MT + mi
                    if n == 0:
                        w("sync", pa_q_sp.get(NCH * MT - 2))
                        w("sync", pa_q_sp.get(NCH * MT - 1))
                        w("sync", pa2_a_sp.get(NCH * MT - 2))
                        w("sync", pa2_a_sp.get(NCH * MT - 1))
                        w("sync", pa2_q1_sp.get(NCH * MT - 2))
                        w("sync", pa2_q1_sp.get(NCH * MT - 1))
                    w("sync", p_sp.get(idx - R4))
                    jj4 = idx - R4
                    if jj4 >= 0:
                        w("sync", ct.get((1, jj4 // MT, jj4 % MT)))
                    w("sync", n1_t.get(idx - R4))
                    w("sync", div_t.get(idx - R4))
                    ldq[idx] = op("sync", lambda e, b=qin[mi % R4], m=mi, nn=n:
                                  e.dma_start(out=b[:], in_=q_d[m * 128:(m + 1) * 128,
                                                              nn * C:(nn + 1) * C]),
                                  sLQ[mi % R4], 16)
                    lda[idx] = op("sync", lambda e, b=ain[mi % R4], m=mi, nn=n:
                                  e.dma_start(out=b[:], in_=arg_d[m * 128:(m + 1) * 128,
                                                                 nn * C:(nn + 1) * C]),
                                  sLA[mi % R4], 16)
                    ldq1[idx] = op("sync", lambda e, b=q1in[mi % R4], m=mi, nn=n:
                                   e.dma_start(out=b[:], in_=q1l_d[m * 128:(m + 1) * 128,
                                                                   nn * C:(nn + 1) * C]),
                                   sLQ1[mi % R4], 16)
                    bank = ps[mi % 4]
                    w("tensor", t_h)
                    if n == 0 and mi == 0:
                        w("tensor", t_wk)
                    w("tensor", ev_k.get(idx - 4))
                    for ki in range(KT):
                        tokm = op("tensor", lambda e, k=ki, m=mi, b=bank, r=hbuf[n % 2]:
                                  e.matmul(b[:], wbuf0[:, k * D + m * 128:k * D + (m + 1) * 128],
                                           r[:, k * C:(k + 1) * C],
                                           start=(k == 0), stop=(k == KT - 1)),
                                  sPe if ki == KT - 1 else None)
                    mm3[(n, mi)] = tokm
                    w("scalar", tokm)
                    w("scalar", k_sp.get(idx - 2))
                    t_evk = op("scalar", lambda e, b=bank, m=mi, r=ko[mi % 2]:
                               e.activation(r[:], b[:], AF.Identity, bias=bias_col(2, m),
                                            scale=1.0), sSc)
                    ev_k[idx] = t_evk
                    w("gpsimd", t_evk)
                    k_sp[idx] = op("gpsimd", lambda e, r=ko[mi % 2], m=mi, nn=n:
                                   e.dma_start(out=k_d[m * 128:(m + 1) * 128,
                                                       nn * C:(nn + 1) * C], in_=r[:]),
                                   sKSP[mi % 2], 16)
                    w("scalar", ldq1[idx])
                    if idx == 0:
                        w("scalar", t_q1prev)
                    op("scalar", lambda e, b=q1in[mi % R4], m=mi:
                       e.activation(b[:], b[:], AF.Ln, bias=prevU[0][:, m:m + 1],
                                    scale=1.0), sSc)
                    t_d = op("scalar", lambda e, b=q1in[mi % R4]:
                             e.activation(b[:], b[:], AF.Exp, bias=0.0, scale=-1.0), sSc)
                    d_t[idx] = t_d
                    w("vector", lda[idx])
                    w("vector", ldq[idx])
                    t_n1 = op("vector", lambda e, a=ain[mi % R4], q_=qin[mi % R4]:
                              e.tensor_mul(q_[:], a[:], q_[:]), sVe)
                    n1_t[idx] = t_n1
                    w("vector", t_d)
                    t_dv = op("vector", lambda e, q_=qin[mi % R4], l=q1in[mi % R4]:
                              e.tensor_mul(q_[:], q_[:], l[:]), sVe)
                    div_t[idx] = t_dv
                    t_s, t_c = scan_stage(1, n, mi, qin[mi % R4], Pc, qin[mi % R4], ct)
                    scand[idx] = t_s
                    if idx > 0:
                        spill_p(idx - 1)
            spill_p(NCH * MT - 1)
            t_pb_carry = ct[(1, NCH - 1, MT - 1)]
            t_pb_pe = tokPe()
            pb_k_sp = dict(k_sp)
            pb_p_sp = dict(p_sp)
            pb_last_lds = [dd[NCH * MT - 4 + i] for dd in (ldq, lda, ldq1)
                           for i in range(4)]

        if upto >= 4:
            t_pprev = pair_exchange(1, Pc, prevU[1], t_pb_carry)
            t_wka = load_w2048(wbuf1, wka, sWka, [t_pa2_pe])

        sLP, sLK = sLDd, sLDe
        sMKSP = [NS("sMKSP0"), NS("sMKSP1")]
        sBSP = [NS("sBSP0"), NS("sBSP1")]
        sK1SP = [NS("sK1SP0"), NS("sK1SP1")]
        if upto >= 4:
            pa = Alloc(nc, OFF_POOL, POOL_SZ, "pc")
            mkb = [pa.t(f"mkb{r}", [128, KT * C], BF16) for r in range(2)]
            R4 = 4
            pin = [pa.t(f"p{r}", [128, C], F32) for r in range(R4)]
            kin = [pa.t(f"k{r}", [128, C], F32) for r in range(R4)]
            kwt = [pa.t(f"kw{r}", [128, C], F32) for r in range(2)]
            k1o = [pa.t(f"k1o{r}", [128, C], F32) for r in range(2)]
            ev_b, b_sp, mk_sp, k1_sp, ct = {}, {}, {}, {}, {}
            mm4, mkv, mkc, kw_t, scand = {}, {}, {}, {}, {}
            ldp, ldk = {}, {}

            def pc_mk_build(n):
                for mi in range(MT):
                    idx = n * MT + mi
                    if idx == 0:
                        guard("sync")
                        w("sync", pb_p_sp.get(NCH * MT - 2))
                        w("sync", pb_p_sp.get(NCH * MT - 1))
                        w("sync", pb_k_sp.get(NCH * MT - 2))
                        w("sync", pb_k_sp.get(NCH * MT - 1))
                        for tk in pb_last_lds:
                            w("sync", tk)
                    w("sync", mkv.get(idx - R4))
                    w("sync", mk_sp.get(idx - R4))
                    w("sync", mkc.get(idx - R4))
                    ldp[idx] = op("sync", lambda e, b=pin[mi % R4], m=mi, nn=n:
                                  e.dma_start(out=b[:], in_=p_d[m * 128:(m + 1) * 128,
                                                              nn * C:(nn + 1) * C]),
                                  sLP[mi % R4], 16)
                    ldk[idx] = op("sync", lambda e, b=kin[mi % R4], m=mi, nn=n:
                                  e.dma_start(out=b[:], in_=k_d[m * 128:(m + 1) * 128,
                                                              nn * C:(nn + 1) * C]),
                                  sLK[mi % R4], 16)
                    w("vector", ldp[idx])
                    w("vector", ldk[idx])
                    if idx == 0:
                        w("vector", t_pprev)
                    t_mk = op("vector", lambda e, p=pin[mi % R4], k_=kin[mi % R4], m=mi:
                              e.scalar_tensor_tensor(k_[:], p[:], prevU[1][:, m:m + 1], k_[:],
                                                     ALU.add, ALU.mult), sVe)
                    mkv[idx] = t_mk
                    w("gpsimd", t_mk)
                    mk_sp[idx] = op("gpsimd", lambda e, o=kin[mi % R4], m=mi, nn=n:
                                    e.dma_start(out=mk_d[m * 128:(m + 1) * 128,
                                                         nn * C:(nn + 1) * C], in_=o[:]),
                                    sMKSP[mi % 2], 16)
                    w("scalar", t_mk)
                    if mi == 0:
                        w("scalar", mm4.get((n - 2, MT - 1)))
                    mkc[idx] = op("scalar", lambda e, s=kin[mi % R4], b=mkb[n % 2], m=mi:
                                  e.activation(b[:, m * C:(m + 1) * C], s[:], AF.Identity,
                                               bias=0.0, scale=1.0), sSc)

            pc_mk_build(0)
            for n in range(NCH):
                if n + 1 < NCH:
                    pc_mk_build(n + 1)
                for mi in range(MT):
                    idx = n * MT + mi
                    bank = ps[mi % 4]
                    w("tensor", mkc[n * MT + MT - 1])
                    if n == 0 and mi == 0:
                        w("tensor", t_wka)
                    w("tensor", ev_b.get(idx - 4))
                    for ki in range(KT):
                        tokm = op("tensor", lambda e, k=ki, m=mi, b=bank, r=mkb[n % 2]:
                                  e.matmul(b[:], wbuf1[:, k * D + m * 128:k * D + (m + 1) * 128],
                                           r[:, k * C:(k + 1) * C],
                                           start=(k == 0), stop=(k == KT - 1)),
                                  sPe if ki == KT - 1 else None)
                    mm4[(n, mi)] = tokm
                    w("scalar", tokm)
                    w("scalar", b_sp.get(idx - 2))
                    w("scalar", scand.get(idx - 2))
                    t_kw = op("scalar", lambda e, b=bank, m=mi, o=kwt[mi % 2]:
                              e.activation(o[:], b[:], AF.Exp,
                                           bias=bka_s[:, m:m + 1], scale=SCALE), sSc)
                    ev_b[idx] = t_kw
                    kw_t[idx] = t_kw
                    w("sync", t_kw)
                    b_sp[idx] = op("sync", lambda e, r=kwt[mi % 2], m=mi, nn=n:
                                   e.dma_start(out=barg_d[m * 128:(m + 1) * 128,
                                                          nn * C:(nn + 1) * C], in_=r[:]),
                                   sBSP[mi % 2], 16)
                    w("vector", t_kw)
                    w("vector", k1_sp.get(idx - 2))
                    t_s, t_c = scan_stage(2, n, mi, kwt[mi % 2], K1c, k1o[mi % 2], ct)
                    scand[idx] = t_s
                    w("sync", t_s)
                    k1_sp[idx] = op("sync", lambda e, o=k1o[mi % 2], m=mi, nn=n:
                                    e.dma_start(out=k1l_d[m * 128:(m + 1) * 128,
                                                          nn * C:(nn + 1) * C], in_=o[:]),
                                    sK1SP[mi % 2], 16)
            t_pc_carry = ct[(2, NCH - 1, MT - 1)]
            t_pc_pe = tokPe()
            pc_mk_sp = dict(mk_sp)
            pc_b_sp = dict(b_sp)
            pc_k1_sp = dict(k1_sp)
            pc_last_lds = [dd[NCH * MT - 4 + i] for dd in (ldp, ldk)
                           for i in range(4)]

        if upto >= 5:
            t_k1prev = pair_exchange(2, K1c, prevU[2], t_pc_carry)
            t_wv = load_w2048(wbuf0, wv, sWv, [t_pb_pe])

        sHL2 = [NS("sHL20"), NS("sHL21")]
        sLMK, sLB, sLK1 = sLDa, sLDb, sLDc
        sVSP = [NS("sVSP0"), NS("sVSP1")]
        sPKSP = [NS("sPKSP0"), NS("sPKSP1")]
        if upto >= 5:
            pa = Alloc(nc, OFF_POOL, POOL_SZ, "pd")
            hbuf2 = [pa.t(f"h{r}", [128, KT * C], BF16) for r in range(2)]
            R4 = 4
            mkin = [pa.t(f"mk{r}", [128, C], F32) for r in range(R4)]
            bain = [pa.t(f"ba{r}", [128, C], F32) for r in range(R4)]
            k1in = [pa.t(f"k1{r}", [128, C], F32) for r in range(R4)]
            vo = [pa.t(f"vo{r}", [128, C], F32) for r in range(2)]
            ev_v, v_sp, pk_sp, ct = {}, {}, {}, {}
            mm5, d2_t, n2_t, div2_t, scand = {}, {}, {}, {}, {}
            ldmk, ldba, ldk1 = {}, {}, {}

            def spill_pk(j):
                jm, jn = j % MT, j // MT
                w("gpsimd", scand[j])
                pk_sp[j] = op("gpsimd", lambda e, o=mkin[jm % R4], m=jm, nn=jn:
                              e.dma_start(out=pk_d[m * 128:(m + 1) * 128,
                                                   nn * C:(nn + 1) * C], in_=o[:]),
                              sPKSP[j % 2], 16)

            for n in range(NCH):
                srch = split3(h_d[:, n * C:(n + 1) * C], KT)
                if n == 0:
                    guard("sync")
                w("sync", mm5.get((n - 2, MT - 1)))
                t_h = op("sync", lambda e, s=srch, b=sb3(hbuf2[n % 2], KT, C):
                         e.dma_start(out=b, in_=s), sHL2[n % 2], 16)
                for mi in range(MT):
                    idx = n * MT + mi
                    if idx == 0:
                        w("sync", pc_mk_sp.get(NCH * MT - 2))
                        w("sync", pc_mk_sp.get(NCH * MT - 1))
                        w("sync", pc_b_sp.get(NCH * MT - 2))
                        w("sync", pc_b_sp.get(NCH * MT - 1))
                        w("sync", pc_k1_sp.get(NCH * MT - 2))
                        w("sync", pc_k1_sp.get(NCH * MT - 1))
                        for tk in pc_last_lds:
                            w("sync", tk)
                    w("sync", pk_sp.get(idx - R4))
                    jj4 = idx - R4
                    if jj4 >= 0:
                        w("sync", ct.get((3, jj4 // MT, jj4 % MT)))
                    w("sync", n2_t.get(idx - R4))
                    w("sync", div2_t.get(idx - R4))
                    ldmk[idx] = op("sync", lambda e, b=mkin[mi % R4], m=mi, nn=n:
                                   e.dma_start(out=b[:], in_=mk_d[m * 128:(m + 1) * 128,
                                                                  nn * C:(nn + 1) * C]),
                                   sLMK[mi % R4], 16)
                    ldba[idx] = op("sync", lambda e, b=bain[mi % R4], m=mi, nn=n:
                                   e.dma_start(out=b[:], in_=barg_d[m * 128:(m + 1) * 128,
                                                                    nn * C:(nn + 1) * C]),
                                   sLB[mi % R4], 16)
                    ldk1[idx] = op("sync", lambda e, b=k1in[mi % R4], m=mi, nn=n:
                                   e.dma_start(out=b[:], in_=k1l_d[m * 128:(m + 1) * 128,
                                                                   nn * C:(nn + 1) * C]),
                                   sLK1[mi % R4], 16)
                    bank = ps[mi % 4]
                    w("tensor", t_h)
                    if n == 0 and mi == 0:
                        w("tensor", t_wv)
                    w("tensor", ev_v.get(idx - 4))
                    for ki in range(KT):
                        tokm = op("tensor", lambda e, k=ki, m=mi, b=bank, r=hbuf2[n % 2]:
                                  e.matmul(b[:], wbuf0[:, k * D + m * 128:k * D + (m + 1) * 128],
                                           r[:, k * C:(k + 1) * C],
                                           start=(k == 0), stop=(k == KT - 1)),
                                  sPe if ki == KT - 1 else None)
                    mm5[(n, mi)] = tokm
                    w("scalar", tokm)
                    w("scalar", v_sp.get(idx - 2))
                    t_evv = op("scalar", lambda e, b=bank, m=mi, r=vo[mi % 2]:
                               e.activation(r[:], b[:], AF.Identity, bias=bias_col(4, m),
                                            scale=1.0), sSc)
                    ev_v[idx] = t_evv
                    w("gpsimd", t_evv)
                    v_sp[idx] = op("gpsimd", lambda e, r=vo[mi % 2], m=mi, nn=n:
                                   e.dma_start(out=v_d[m * 128:(m + 1) * 128,
                                                       nn * C:(nn + 1) * C], in_=r[:]),
                                   sVSP[mi % 2], 16)
                    w("scalar", ldk1[idx])
                    if idx == 0:
                        w("scalar", t_k1prev)
                    op("scalar", lambda e, b=k1in[mi % R4], m=mi:
                       e.activation(b[:], b[:], AF.Ln, bias=prevU[2][:, m:m + 1],
                                    scale=1.0), sSc)
                    t_d2 = op("scalar", lambda e, b=k1in[mi % R4]:
                              e.activation(b[:], b[:], AF.Exp, bias=0.0, scale=-1.0), sSc)
                    d2_t[idx] = t_d2
                    w("vector", ldba[idx])
                    w("vector", ldmk[idx])
                    t_n2 = op("vector", lambda e, a=bain[mi % R4], mkk=mkin[mi % R4]:
                              e.tensor_mul(mkk[:], a[:], mkk[:]), sVe)
                    n2_t[idx] = t_n2
                    w("vector", t_d2)
                    t_dv = op("vector", lambda e, mkk=mkin[mi % R4], l=k1in[mi % R4]:
                              e.tensor_mul(mkk[:], mkk[:], l[:]), sVe)
                    div2_t[idx] = t_dv
                    t_s, t_c = scan_stage(3, n, mi, mkin[mi % R4], PKc, mkin[mi % R4], ct)
                    scand[idx] = t_s
                    if idx > 0:
                        spill_pk(idx - 1)
            spill_pk(NCH * MT - 1)
            t_pd_carry = ct[(3, NCH - 1, MT - 1)]
            t_pd_pe = tokPe()
            pd_v_sp = dict(v_sp)
            pd_pk_sp = dict(pk_sp)

        if upto >= 6:
            t_pkprev = pair_exchange(3, PKc, prevU[3], t_pd_carry)

        sSLAB = [NS("sSLAB0"), NS("sSLAB1")]

        def load_slab(s_, reg, pre):
            for tk in pre:
                w("gpsimd", tk)
            w1s = split3(w1[:, s_ * 1024:(s_ + 1) * 1024], KT)
            w3s = split3(w3[:, s_ * 1024:(s_ + 1) * 1024], KT)
            op("gpsimd", lambda e, ss=w1s, d=sb3(pgW[reg][0], KT, 1024):
               e.dma_start(out=d, in_=ss), sSLAB[reg], 16)
            return op("gpsimd", lambda e, ss=w3s, d=sb3(pgW[reg][1], KT, 1024):
                      e.dma_start(out=d, in_=ss), sSLAB[reg], 16)

        sLX = [NS("sLX0"), NS("sLX1")]
        sLV = [NS("sLV0"), NS("sLV1")]
        sLPK = [NS("sLPK0"), NS("sLPK1")]
        sX1SP = NS("sX1SP")
        sH2SP = [NS("sH2SP0"), NS("sH2SP1")]
        sHIDSP = [NS("sHIDSP0"), NS("sHIDSP1")]
        if upto >= 6:
            t_slab = {0: load_slab(0, 0, [t_pd_pe])}
            pa = Alloc(nc, OFF_POOL, POOL_SZ, "pf")
            x1b = pa.t("x1", [128, KT * C], F32)
            xin = [pa.t(f"x{r}", [128, C], F32) for r in range(2)]
            vin = [pa.t(f"v{r}", [128, C], F32) for r in range(2)]
            pkin = [pa.t(f"pk{r}", [128, C], F32) for r in range(2)]
            att = [pa.t(f"at{r}", [128, C], F32) for r in range(2)]
            sq2 = [pa.t(f"sq{r}", [128, C], BF16) for r in range(2)]
            rows = {n_: pa.t(n_, [1, C], F32) for n_ in ("mu", "ms", "var", "rr", "sr")}
            h2c = [nc.alloc_sbuf_tensor_at(f"h2c{r}", [128, KT * C], BF16,
                                           offset=OFF_W1 + r * 16384) for r in range(2)]
            sg0 = [nc.alloc_sbuf_tensor_at(f"sg0{r}", [128, C], F32,
                                           offset=OFF_W1 + 32768 + r * 2048)
                   for r in range(2)]
            hi0 = [nc.alloc_sbuf_tensor_at(f"hi0{r}", [128, C], BF16,
                                           offset=OFF_W1 + 36864 + r * 1024)
                   for r in range(2)]
            at_t, x1_t, h2_t, x1_sp, h2_sp, sq_rd = {}, {}, {}, {}, {}, {}
            hid_sp, mm6, sg_t, hi_t = {}, {}, {}, {}
            ldx, ldv, ldpk = {}, {}, {}
            prev_ln_reads = None
            prev_stats_read = None
            for n in range(NCH):
                for mi in range(MT):
                    idx = n * MT + mi
                    if idx == 0:
                        guard("sync")
                        w("sync", pd_v_sp.get(NCH * MT - 2))
                        w("sync", pd_v_sp.get(NCH * MT - 1))
                        w("sync", pd_pk_sp.get(NCH * MT - 2))
                        w("sync", pd_pk_sp.get(NCH * MT - 1))
                    w("sync", at_t.get(idx - 2))
                    w("sync", x1_t.get(idx - 2))
                    ldx[idx] = op("sync", lambda e, b=xin[mi % 2], m=mi, nn=n:
                                  e.dma_start(out=b[:], in_=xT[m * 128:(m + 1) * 128,
                                                              nn * C:(nn + 1) * C]),
                                  sLX[mi % 2], 16)
                    ldv[idx] = op("sync", lambda e, b=vin[mi % 2], m=mi, nn=n:
                                  e.dma_start(out=b[:], in_=v_d[m * 128:(m + 1) * 128,
                                                              nn * C:(nn + 1) * C]),
                                  sLV[mi % 2], 16)
                    ldpk[idx] = op("sync", lambda e, b=pkin[mi % 2], m=mi, nn=n:
                                   e.dma_start(out=b[:], in_=pk_d[m * 128:(m + 1) * 128,
                                                                  nn * C:(nn + 1) * C]),
                                   sLPK[mi % 2], 16)
                    w("vector", ldpk[idx])
                    w("vector", ldv[idx])
                    if idx == 0:
                        w("vector", t_pkprev)
                    w("vector", h2_t.get(idx - 2))
                    t_at = op("vector", lambda e, p=pkin[mi % 2], v_=vin[mi % 2],
                              o=att[mi % 2], m=mi:
                              e.scalar_tensor_tensor(o[:], p[:], prevU[3][:, m:m + 1], v_[:],
                                                     ALU.add, ALU.mult), sVe)
                    at_t[idx] = t_at
                    w("vector", ldx[idx])
                    if n > 0 and mi == 0:
                        w("vector", x1_sp.get(n - 1))
                        w("vector", prev_stats_read)
                    t_x1 = op("vector", lambda e, a=att[mi % 2], x=xin[mi % 2], m=mi:
                              e.tensor_add(x1b[:, m * C:(m + 1) * C], a[:], x[:]), sVe)
                    x1_t[idx] = t_x1
                w("sync", x1_t[n * MT + MT - 1])
                w("sync", x1_sp.get(n - 1))
                dstx = split3(x1_d[:, n * C:(n + 1) * C], KT)
                x1_sp[n] = op("sync", lambda e, d=dstx, s_=sb3(x1b, KT, C):
                              e.dma_start(out=d, in_=s_), sX1SP, 16)
                w("tensor", prev_ln_reads)
                for mi in range(MT):
                    w("scalar", last2(sq_rd, (n, mi - 2), (n - 1, mi + MT - 2)))
                    w("scalar", x1_t[n * MT + mi])
                    t_sq = op("scalar", lambda e, m=mi, r=sq2[mi % 2]:
                              e.activation(r[:], x1b[:, m * C:(m + 1) * C], AF.Square,
                                           bias=0.0, scale=1.0), sSc)
                    w("tensor", x1_t[n * MT + mi])
                    op("tensor", lambda e, m=mi: e.matmul(
                        ps[2][0:1, 0:C], ones_cf[:, :], x1b[:, m * C:(m + 1) * C],
                        start=(m == 0), stop=(m == MT - 1)), sPe)
                    w("tensor", t_sq)
                    t_stats = op("tensor", lambda e, m=mi, r=sq2[mi % 2]:
                                 e.matmul(ps[3][0:1, 0:C], ones_cb[:, :], r[:],
                                          start=(m == 0), stop=(m == MT - 1)), sPe)
                    sq_rd[(n, mi)] = t_stats
                prev_stats_read = t_stats
                t_bc = ln_rows_and_bcast(rows, True, t_stats, prev_ln_reads)
                for mi in range(MT):
                    idx = n * MT + mi
                    w("vector", t_bc)
                    if idx == 0:
                        w("vector", t_pc_pe)
                    w("vector", h2_sp.get(idx - 2 * MT))
                    w("vector", mm6.get((0, n - 2, SMT - 1)))
                    op("vector", lambda e, m=mi, r=att[mi % 2]:
                       e.tensor_mul(r[:], x1b[:, m * C:(m + 1) * C], ps[4][:, 0:C]), sVe)
                    t_h2 = op("vector", lambda e, a=att[mi % 2], hc=h2c[n % 2], m=mi:
                              e.tensor_sub(hc[:, m * C:(m + 1) * C], a[:],
                                           ps[5][:, 0:C]), sVe)
                    h2_t[idx] = t_h2
                    w("sync", t_h2)
                    h2_sp[idx] = op("sync", lambda e, hc=h2c[n % 2], m=mi, nn=n:
                                    e.dma_start(out=h2_d[m * 128:(m + 1) * 128,
                                                         nn * C:(nn + 1) * C],
                                                in_=hc[:, m * C:(m + 1) * C]),
                                    sH2SP[mi % 2], 16)
                prev_ln_reads = tokVe()
                for mi in range(SMT):
                    j = n * SMT + mi
                    gbank = ps[mi % 2]
                    ubank = ps[6 + mi % 2]
                    if mi == 0:
                        w("tensor", h2_t[n * MT + MT - 1])
                    if n == 0 and mi == 0:
                        w("tensor", t_slab[0])
                    w("tensor", sg_t.get((0, j - 2)))
                    w("tensor", hi_t.get((0, j - 2)))
                    for ki in range(KT):
                        t_g = op("tensor", lambda e, k=ki, m=mi, b=gbank, r=h2c[n % 2]:
                                 e.matmul(b[:], pgW[0][0][:, k * 1024 + m * 128:
                                                          k * 1024 + (m + 1) * 128],
                                          r[:, k * C:(k + 1) * C],
                                          start=(k == 0), stop=(k == KT - 1)),
                                 sPe if ki == KT - 1 else None)
                    for ki in range(KT):
                        tokm = op("tensor", lambda e, k=ki, m=mi, b=ubank, r=h2c[n % 2]:
                                  e.matmul(b[:], pgW[0][1][:, k * 1024 + m * 128:
                                                           k * 1024 + (m + 1) * 128],
                                           r[:, k * C:(k + 1) * C],
                                           start=(k == 0), stop=(k == KT - 1)),
                                  sPe if ki == KT - 1 else None)
                    mm6[(0, n, mi)] = tokm
                    w("scalar", t_g)
                    w("scalar", hi_t.get((0, j - 2)))
                    t_sg = op("scalar", lambda e, b=gbank, o=sg0[mi % 2]:
                              e.activation(o[:], b[:], AF.Silu, bias=0.0, scale=1.0), sSc)
                    sg_t[(0, j)] = t_sg
                    w("vector", t_sg)
                    w("vector", tokm)
                    w("vector", hid_sp.get((0, j - 2)))
                    t_hi = op("vector", lambda e, sg=sg0[mi % 2], b=ubank, o=hi0[mi % 2]:
                              e.tensor_mul(o[:], sg[:], b[:]), sVe)
                    hi_t[(0, j)] = t_hi
                    w("gpsimd", t_hi)
                    hid_sp[(0, j)] = op(
                        "gpsimd", lambda e, o=hi0[mi % 2], m=mi, nn=n:
                        e.dma_start(out=hid_d[m * 128:(m + 1) * 128,
                                              nn * C:(nn + 1) * C], in_=o[:]),
                        sHIDSP[mi % 2], 16)
            pf_x1_sp = dict(x1_sp)
            pf_h2_sp = dict(h2_sp)
            t_pg0_pe = tokPe()

        sH2L = [NS("sH2L0"), NS("sH2L1")]
        if upto >= 7:
            t_slab[1] = load_slab(1, 1, [t_pg0_pe, hi_t[(0, NCH * SMT - 1)],
                                         hid_sp[(0, NCH * SMT - 2)],
                                         hid_sp[(0, NCH * SMT - 1)]])
            pa = Alloc(nc, OFF_POOL, POOL_SZ, "pg")
            h2b = [pa.t(f"h2{r}", [128, KT * C], BF16) for r in range(2)]
            sgt = [pa.t(f"sg{r}", [128, C], F32) for r in range(2)]
            hio = [pa.t(f"hi{r}", [128, C], BF16) for r in range(2)]
            first_load = True
            for s_ in range(1, NSLAB):
                reg = s_ % 2
                if s_ + 1 < NSLAB:
                    t_slab[s_ + 1] = load_slab(
                        s_ + 1, (s_ + 1) % 2,
                        [mm6.get((s_ - 1, NCH - 1, SMT - 1))])
                for n in range(NCH):
                    if first_load:
                        guard("sync")
                        w("sync", pf_h2_sp.get(NCH * MT - 2))
                        w("sync", pf_h2_sp.get(NCH * MT - 1))
                        first_load = False
                    key2 = (s_, n - 2, SMT - 1) if n >= 2 else (s_ - 1, NCH - 2 + n, SMT - 1)
                    w("sync", mm6.get(key2))
                    srch = split3(h2_d[:, n * C:(n + 1) * C], KT)
                    t_h2l = op("sync", lambda e, sc_=srch, b=sb3(h2b[n % 2], KT, C):
                               e.dma_start(out=b, in_=sc_), sH2L[n % 2], 16)
                    for mi in range(SMT):
                        j = n * SMT + mi
                        gbank = ps[mi % 2]
                        ubank = ps[2 + mi % 2]
                        w("tensor", t_h2l)
                        if n == 0 and mi == 0:
                            w("tensor", t_slab[s_])
                        w("tensor", sg_t.get((s_, j - 2)))
                        w("tensor", hi_t.get((s_, j - 2)))
                        for ki in range(KT):
                            t_g = op("tensor", lambda e, k=ki, m=mi, b=gbank, r=h2b[n % 2],
                                     rr=reg:
                                     e.matmul(b[:], pgW[rr][0][:, k * 1024 + m * 128:
                                                               k * 1024 + (m + 1) * 128],
                                              r[:, k * C:(k + 1) * C],
                                              start=(k == 0), stop=(k == KT - 1)),
                                     sPe if ki == KT - 1 else None)
                        for ki in range(KT):
                            tokm = op("tensor", lambda e, k=ki, m=mi, b=ubank, r=h2b[n % 2],
                                      rr=reg:
                                      e.matmul(b[:], pgW[rr][1][:, k * 1024 + m * 128:
                                                                k * 1024 + (m + 1) * 128],
                                               r[:, k * C:(k + 1) * C],
                                               start=(k == 0), stop=(k == KT - 1)),
                                      sPe if ki == KT - 1 else None)
                        mm6[(s_, n, mi)] = tokm
                        w("scalar", t_g)
                        w("scalar", hi_t.get((s_, j - 2)))
                        t_sg = op("scalar", lambda e, b=gbank, o=sgt[mi % 2]:
                                  e.activation(o[:], b[:], AF.Silu, bias=0.0, scale=1.0), sSc)
                        sg_t[(s_, j)] = t_sg
                        w("vector", t_sg)
                        w("vector", tokm)
                        w("vector", hid_sp.get((s_, j - 2)))
                        t_hi = op("vector", lambda e, sg=sgt[mi % 2], b=ubank, o=hio[mi % 2]:
                                  e.tensor_mul(o[:], sg[:], b[:]), sVe)
                        hi_t[(s_, j)] = t_hi
                        w("sync", t_hi)
                        hid_sp[(s_, j)] = op(
                            "sync", lambda e, o=hio[mi % 2], ss=s_, m=mi, nn=n:
                            e.dma_start(out=hid_d[ss * 1024 + m * 128:
                                                  ss * 1024 + (m + 1) * 128,
                                                  nn * C:(nn + 1) * C], in_=o[:]),
                            sHIDSP[mi % 2], 16)
            t_pg_pe = tokPe()
            pg_hid_sp = dict(hid_sp)

        sHIB = [NS("sHIBa"), NS("sHIBb")]
        sHIB2 = [NS("sHIB2a"), NS("sHIB2b")]
        sW2L = [NS("sW2L0"), NS("sW2L1")]
        sLX1 = [NS("sLX10"), NS("sLX11")]
        sYSP = [NS("sYSP0"), NS("sYSP1")]
        if upto >= 8:
            def load_mg(mg, reg, pre):
                for tk in pre:
                    w("gpsimd", tk)
                w("gpsimd", t_ag)
                src = split3(w2pp_all[mg * HF:(mg + 1) * HF, :], GT)
                return op("gpsimd", lambda e, s_=src, d=sb3(phW[reg], GT, 256):
                          e.dma_start(out=d, in_=s_), sW2L[reg], 16)

            t_mg = {0: load_mg(0, 0, [t_pg_pe])}
            pa = Alloc(nc, OFF_POOL, POOL_SZ, "ph")
            hibP = [pa.t(f"hibP{r}", [128, 32 * C], BF16) for r in range(2)]
            hibW = [nc.alloc_sbuf_tensor_at(f"hibW{r}", [128, 32 * C], BF16,
                                            offset=[OFF_W0, OFF_W1][r] + 32768)
                    for r in range(2)]
            x1in = [pa.t(f"x1{r}", [128, C], F32) for r in range(2)]
            oo = [pa.t(f"oo{r}", [128, C], F32) for r in range(2)]
            mm7, y_sp, oo_t, ev7, ldx1 = {}, {}, {}, {}, {}
            t_hlA = {}
            t_hlB = {}
            for n in range(NCH):
                if n == 0:
                    guard("sync")
                    w("sync", pg_hid_sp.get((NSLAB - 1, NCH * SMT - 2)))
                    w("sync", pg_hid_sp.get((NSLAB - 1, NCH * SMT - 1)))
                    w("sync", t_pg_pe)
                for nn2 in ([0, 1] if n == 0 else ([n + 1] if n + 1 < NCH else [])):
                    srch = split3(hid_d[:, nn2 * C:(nn2 + 1) * C], GT)
                    w("sync", mm7.get((nn2 - 2, NMG - 1, 1)))
                    t_hlA[nn2] = op("sync", lambda e, s_=srch[:, 0:32, :],
                                    d=sb3(hibP[nn2 % 2], 32, C):
                                    e.dma_start(out=d, in_=s_), sHIB[nn2 % 2], 16)
                    t_hlB[nn2] = op("sync", lambda e, s_=srch[:, 32:64, :],
                                    d=sb3(hibW[nn2 % 2], 32, C):
                                    e.dma_start(out=d, in_=s_), sHIB2[nn2 % 2], 16)
                t_hl1 = t_hlA[n]
                t_hl = t_hlB[n]
                for mg in range(NMG):
                    reg = mg % 2
                    gidx = n * NMG + mg
                    if mg + 1 < NMG or n + 1 < NCH:
                        nxt = mg + 1 if mg + 1 < NMG else 0
                        prev_key = (n, mg - 1, 1) if mg >= 1 else (n - 1, NMG - 1, 1)
                        t_mg[gidx + 1] = load_mg(nxt, (mg + 1) % 2,
                                                 [mm7.get(prev_key) or t_pg_pe])
                    w("tensor", t_mg[gidx])
                    if mg == 0:
                        w("tensor", t_hl1)
                    else:
                        w("tensor", t_hl)
                    w("tensor", ev7.get(gidx - 4))
                    for ml in range(2):
                        bank = ps[(mg % 4) * 2 + ml]
                        for ki in range(GT):
                            if mg == 0 and ml == 0 and ki == 32:
                                w("tensor", t_hl)
                            hsrc = hibP[n % 2] if ki < 32 else hibW[n % 2]
                            ko_ = ki if ki < 32 else ki - 32
                            tokm = op("tensor", lambda e, k=ki, k2=ko_, m=ml, b=bank, r=reg,
                                      hs=hsrc:
                                      e.matmul(b[:], phW[r][:, k * 256 + m * 128:
                                                            k * 256 + (m + 1) * 128],
                                               hs[:, k2 * C:(k2 + 1) * C],
                                               start=(k == 0), stop=(k == GT - 1)),
                                      sPe if ki == GT - 1 else None)
                        mm7[(n, mg, ml)] = tokm
                    for ml in range(2):
                        j = gidx * 2 + ml
                        mi = mg * 2 + ml
                        bank = ps[(mg % 4) * 2 + ml]
                        obuf = oo[j % 2]
                        w("scalar", mm7[(n, mg, ml)])
                        w("scalar", y_sp.get(j - 2))
                        t_ev = op("scalar", lambda e, b=bank, o=obuf:
                                  e.activation(o[:], b[:], AF.Identity, bias=0.0,
                                               scale=1.0), sSc)
                        ev7[gidx] = t_ev
                        if n == 0 and mg == 0 and ml == 0:
                            w("sync", pf_x1_sp.get(NCH - 1))
                        w("sync", oo_t.get(j - 2))
                        ldx1[j] = op("sync", lambda e, b=x1in[ml], m=mi, nn=n:
                                     e.dma_start(out=b[:], in_=x1_d[m * 128:(m + 1) * 128,
                                                                    nn * C:(nn + 1) * C]),
                                     sLX1[ml], 16)
                        w("vector", t_ev)
                        w("vector", ldx1[j])
                        t_y = op("vector", lambda e, o=obuf, x=x1in[ml]:
                                 e.tensor_add(o[:], o[:], x[:]), sVe)
                        oo_t[j] = t_y
                        w("gpsimd", t_y)
                        w("gpsimd", y_sp.get(j - 2))
                        y_sp[j] = op("gpsimd", lambda e, o=obuf, m=mi, nn=n:
                                     e.dma_start(out=y[m * 128:(m + 1) * 128,
                                                       nn * C:(nn + 1) * C], in_=o[:]),
                                     sYSP[j % 2], 16)
            w("gpsimd", y_sp[NCH * NMG * 2 - 2])
            w("gpsimd", y_sp[NCH * NMG * 2 - 1])

        if taps:
            guard("gpsimd")
            for tname in taps:
                src = scr_map[tname]
                dst = tap_outs[tname]
                t_t = op("gpsimd", lambda e, s_=src.tensor.ap(), d=dst.tensor.ap():
                         e.dma_start(out=d, in_=s_), sCcD, 16)
            w("gpsimd", t_t)

        @block.tensor
        def _(t):
            for f in streams["tensor"]:
                f(t)

        @block.vector
        def _(v):
            for f in streams["vector"]:
                f(v)

        @block.scalar
        def _(s):
            for f in streams["scalar"]:
                f(s)

        @block.sync
        def _(s):
            for f in streams["sync"]:
                f(s)

        @block.gpsimd
        def _(g):
            for f in streams["gpsimd"]:
                f(g)

    return nc


def _hadamard128():
    h = np.array([[1.0]], dtype=np.float32)
    while h.shape[0] < 128:
        h = np.block([[h, h], [h, -h]])
    return h.astype(np.float32)


def make_in_maps(inputs):
    x = np.asarray(inputs["L0_attn_hidden_states"], np.float32)
    WqT = np.ascontiguousarray(np.asarray(inputs["L0_attn_Wq"], np.float32).T)
    WqaT = np.ascontiguousarray(np.asarray(inputs["L0_attn_Wqa"], np.float32).T)
    WkT = np.ascontiguousarray(np.asarray(inputs["L0_attn_Wk"], np.float32).T)
    WkaT = np.ascontiguousarray(np.asarray(inputs["L0_attn_Wka"], np.float32).T)
    WvT = np.ascontiguousarray(np.asarray(inputs["L0_attn_Wv"], np.float32).T)
    W1T = np.ascontiguousarray(np.asarray(inputs["L0_ffn_W1"], np.float32).T)
    W3T = np.ascontiguousarray(np.asarray(inputs["L0_ffn_W3"], np.float32).T)
    W2T = np.ascontiguousarray(np.asarray(inputs["L0_ffn_W2"], np.float32).T)
    h128 = _hadamard128()

    def col16(v):
        return np.ascontiguousarray(np.asarray(v, np.float32).reshape(16, 128).T)

    bq = col16(inputs["L0_attn_bq"]); bqa = col16(inputs["L0_attn_bqa"])
    bk = col16(inputs["L0_attn_bk"]); bka = col16(inputs["L0_attn_bka"])
    bv = col16(inputs["L0_attn_bv"])
    lng = col16(inputs["L0_attn_ln_g"]); lnb = col16(inputs["L0_attn_ln_b"])

    in_maps = []
    for c in range(8):
        b, hf = c >> 1, c & 1
        xTc = np.ascontiguousarray(x[b, hf * T:(hf + 1) * T, :].T)
        w2sc = np.ascontiguousarray(W2T[:, c * 256:(c + 1) * 256])
        cc = np.zeros((128, 114), np.float32)
        cc[:, 0:16] = bq; cc[:, 16:32] = bqa; cc[:, 32:48] = bk
        cc[:, 48:64] = bka; cc[:, 64:80] = bv
        cc[:, 80:96] = lng; cc[:, 96:112] = lnb
        cc[:, 112] = 1.0 - hf
        cc[:, 113] = float(hf)
        in_maps.append({
            "xT": xTc, "wq": WqT, "wqa": WqaT, "wk": WkT, "wka": WkaT, "wv": WvT,
            "w1": W1T, "w3": W3T, "w2s": w2sc, "consts": cc, "h128": h128,
        })
    return in_maps


def kernel(**inputs):
    nc = build_program(upto=UPTO, taps=TAPS)
    in_maps = make_in_maps(inputs)
    res = run_bass_kernel_spmd(nc, in_maps, core_ids=list(range(8)))
    out = np.empty((B, S, D), np.float32)
    for c in range(8):
        b, hf = c >> 1, c & 1
        out[b, hf * T:(hf + 1) * T, :] = res.results[c]["y"].T
    if TAPS:
        kernel.tap_results = res.results
    return out
